# version 02
# baseline (speedup 1.0000x reference)
import math
import os as _os
import contextlib
import numpy as np
import concourse.bass as bass
import concourse.mybir as mybir
from concourse.bass_utils import run_bass_kernel_spmd

F32 = mybir.dt.float32
BF16 = mybir.dt.bfloat16
OP = mybir.AluOpType
AF = mybir.ActivationFunctionType

B, S, E = 4, 2048, 2048
H, LAT, D, R = 16, 512, 128, 64
TOK = 1024
SHIFT = 12.0
SCALE = 1.0 / math.sqrt(D)
EPS = 1e-5
AC = 88064

O_XLNFM, O_WKV, O_WQD, O_WRKH = 0, 32768, 40960, 49152
O_XLT, O_KRT = 57344, 61440
O_CC, O_SGN = 63488, 65536
O_KW, O_VW, O_QW = 0, 16384, 32896
O_PT0, O_PT1, O_UPS = 41088, 45184, 49280
O_KV, O_QD, O_ATTN = 57472, 65664, 69760
O_O2SB, O_QRT = 86144, 86400
O_WOA, O_X2, O_WOB = 0, 16384, 49152
O_X2LN, O_WFFN = 0, 49152
O_WA, O_WB, O_WR1, O_HSB = 65536, 69632, 73728, 74752
O_CEXP, O_XA, O_WXA = 75776, 77824, 79872
O_LGT, O_CFM = 81920, 83968
O_CFMB, O_TOKB = 73728, 86016

BC_KV, BC_QD, BC_KU, BC_QU, BC_RK, BC_RQ = 0, 4, 8, 24, 40, 48
BC_R1, BC_R2, BC_G = 64, 65, 66

ENGS = ("sync", "gpsimd", "tensor", "vector", "scalar")
SEM_OF = dict(sync="sd", gpsimd="sg", tensor="t", vector="v", scalar="s")


def build(debug=False):
    PH = int(_os.environ.get("BASS_PH", "99"))
    nc = bass.Bass("TRN2", target_bir_lowering=False)

    xc = nc.dram_tensor("xc", [S, E], F32, kind="ExternalInput").ap()
    cc_in = nc.dram_tensor("cc", [128, S], F32, kind="ExternalInput").ap()
    sgn_in = nc.dram_tensor("sgn", [128, S], F32, kind="ExternalInput").ap()
    identf_in = nc.dram_tensor("identf", [128, 128], F32, kind="ExternalInput").ap()
    bias_pp_in = nc.dram_tensor("bias_pp", [128, 68], F32, kind="ExternalInput").ap()
    brv_in = nc.dram_tensor("brv", [1, E], F32, kind="ExternalInput").ap()
    bro_in = nc.dram_tensor("bro", [1, E], F32, kind="ExternalInput").ap()
    brf_in = nc.dram_tensor("brf", [1, E], F32, kind="ExternalInput").ap()
    emat_in = nc.dram_tensor("emat", [16, 256], F32, kind="ExternalInput").ap()
    rotm_in = nc.dram_tensor("rotm", [128, 128], F32, kind="ExternalInput").ap()
    w_in = {}
    for name, shp in [("wkv", [128, 8192]), ("wqd", [128, 8192]),
                      ("wrk", [128, 16384]), ("wku", [128, 4096]),
                      ("wqu", [128, 4096]), ("wvu", [128, 8192]),
                      ("wrq", [128, 4096]), ("woa", [128, 16384]),
                      ("wob", [128, 16384]), ("wfa", [128, 16384]),
                      ("wfb", [128, 16384]), ("wa", [128, 4096]),
                      ("wb", [128, 4096]), ("wr1", [128, 1024]),
                      ("wr2", [64, 32])]:
        w_in[name] = nc.dram_tensor(name, shp, F32, kind="ExternalInput").ap()
    y = nc.dram_tensor("y", [TOK, E], F32, kind="ExternalOutput").ap()
    kv_hbm = nc.dram_tensor("kv_hbm", [128, 8192], BF16).ap()
    qd_hbm = nc.dram_tensor("qd_hbm", [128, 4096], BF16).ap()
    kr_hbm = nc.dram_tensor("kr_hbm", [128, 16384], BF16).ap()

    dbg = {}
    _dbg_shapes = {"d_xlnfm": ([128, 32768], BF16), "d_kv": ([128, 8192], BF16),
                   "d_qd": ([128, 4096], BF16), "d_kr": ([128, 16384], BF16),
                   "d_kw": ([128, 16384], BF16), "d_vw": ([128, 16512], BF16),
                   "d_qw": ([128, 8192], BF16), "d_attn": ([128, 16384], BF16),
                   "d_x2": ([128, 16384], F32), "d_x2ln": ([128, 16384], BF16),
                   "d_lgt": ([128, 128], F32), "d_coef": ([128, 128], F32),
                   "d_xa": ([128, 2048], BF16), "d_wxa": ([128, 2048], BF16),
                   "d_h": ([128, 1024], BF16), "d_xf0": ([128, 2048], F32),
                   "d_xlt": ([128, 4096], BF16), "d_lnx": ([128, 10], F32)}

    class _Dbg(dict):
        def __getitem__(self, k):
            if k not in self:
                shp, dt = _dbg_shapes[k]
                self[k] = nc.dram_tensor(k, shp, dt, kind="ExternalOutput").ap()
            return dict.__getitem__(self, k)
    dbg = _Dbg()
    if debug:
        lvl_names = {1: ["d_xlnfm", "d_xf0", "d_xlt"],
                     2: ["d_kv", "d_qd", "d_kr"],
                     3: ["d_kw", "d_vw", "d_qw", "d_attn"],
                     4: ["d_x2"], 5: ["d_x2ln"],
                     6: ["d_lgt", "d_coef", "d_xa", "d_wxa", "d_h"]}
        for lv in range(1, int(debug) + 1):
            for nm in lvl_names.get(lv, []):
                dbg[nm]

    ctx = contextlib.ExitStack()
    en = ctx.enter_context
    with ctx:
        ar = en(nc.sbuf_tensor("arena", [128, AC], BF16))
        ident16 = en(nc.sbuf_tensor("ident16", [128, 128], BF16))
        identf = en(nc.sbuf_tensor("identf_s", [128, 128], F32))
        ccq = en(nc.sbuf_tensor("ccq", [128, TOK], BF16))
        sgnq = en(nc.sbuf_tensor("sgnq", [128, TOK], BF16))
        brv16 = en(nc.sbuf_tensor("brv16", [1, E], BF16))
        bro16 = en(nc.sbuf_tensor("bro16", [1, E], BF16))
        brf16 = en(nc.sbuf_tensor("brf16", [1, E], BF16))
        bias_pp = en(nc.sbuf_tensor("bias_pp_s", [128, 68], F32))
        emat16 = en(nc.sbuf_tensor("emat16", [16, 256], BF16))
        ones16 = en(nc.sbuf_tensor("ones16", [1, 128], BF16))
        wr2_sb = en(nc.sbuf_tensor("wr2_sb", [64, 32], BF16))
        rot16 = en(nc.sbuf_tensor("rot16", [128, 128], BF16))
        consts = en(nc.sbuf_tensor("consts", [128, 8], F32))
        C_EPS = consts[:, 0:1]
        C_NSH = consts[:, 1:2]
        C_SCL = consts[:, 2:3]
        C_NEG2 = consts[:, 3:4]
        C_ONE = consts[:, 4:5]
        C_NEG1 = consts[:, 5:6]
        xf = [en(nc.sbuf_tensor(f"xf{i}", [128, E], F32)) for i in range(2)]
        lnst = en(nc.sbuf_tensor("lnst", [128, 48], F32))
        lnag = en(nc.sbuf_tensor("lnag", [128, 4], F32))
        lnnm = en(nc.sbuf_tensor("lnnm", [128, 2], F32))
        lnri = en(nc.sbuf_tensor("lnri", [128, 2], F32))
        lnr = en(nc.sbuf_tensor("lnr", [128, 2], F32))
        lnr2 = en(nc.sbuf_tensor("lnr2", [128, 2], F32))
        zin = en(nc.sbuf_tensor("zin", [128, 2], F32))
        lgt_tok = en(nc.sbuf_tensor("lgt_tok", [128, 128], F32))
        m8 = en(nc.sbuf_tensor("m8", [128, 8], F32))
        tks = en(nc.sbuf_tensor("tks", [128, 4], F32))
        msk = en(nc.sbuf_tensor("msk", [128, 64], F32))
        junk = en(nc.sbuf_tensor("junk", [128, 8], F32))
        _jc = [0]
        coef_tok = en(nc.sbuf_tensor("coef_tok", [128, 128], F32))
        pA = en(nc.psum_tensor("pA", [128, 2048], F32))
        pB = en(nc.psum_tensor("pB", [128, 2048], F32))
        sems = {k: en(nc.semaphore(k)) for k in ("t", "v", "s")}
        NDQ = 90
        for li in range(NDQ):
            sems[f"dq{li}"] = en(nc.semaphore(f"dq{li}"))
        blk = en(nc.Block("all"))

        ops = {e_: [] for e_ in ENGS}
        cnt = {s_: 0 for s_ in ("t", "v", "s")}
        dq_next = [0, 0]
        lastt = {e_: None for e_ in ENGS}

        PH_OFF = [False]

        def _flat(waits):
            out = []
            for w in waits:
                if w is None:
                    continue
                if isinstance(w, list):
                    out.extend(w)
                else:
                    out.append(w)
            return out

        def jop(v):
            _jc[0] = (_jc[0] + 1) % 8
            return v.memset(junk[:, _jc[0]:_jc[0] + 1], 0.0)

        def add(eng, fn, waits=(), n_dma=None):
            if PH_OFF[0]:
                return None
            ws = _flat(waits)
            if n_dma is not None:
                plan = []
                tok = []
                for _ in range(n_dma):
                    if eng == "gpsimd":
                        li = 58 + dq_next[1]
                        dq_next[1] += 1
                        assert li < 90, "out of gpsimd dma sems"
                        rnd = 0
                    else:
                        li = dq_next[0] % 58
                        rnd = dq_next[0] // 58
                        dq_next[0] += 1
                    lane = f"dq{li}"
                    plan.append((lane, 0))
                    tok.append((lane, 16 * (rnd + 1)))
                ops[eng].append(("dma", fn, ws, plan))
            else:
                sn = SEM_OF[eng]
                selfw = cnt[sn] if eng in ("vector", "scalar") else None
                cnt[sn] += 1
                tok = (sn, cnt[sn])
                ops[eng].append(("cmp", fn, ws, selfw))
            lastt[eng] = tok
            return tok

        def fence():
            toks = _flat([lastt[e_] for e_ in ("tensor", "vector", "scalar")])
            for e_ in ENGS:
                ops[e_].append(("w", None, list(toks), None))

        def b16(off, n):
            assert off + n <= AC
            return ar[:, off:off + n]

        def f32v(off, nf):
            assert off % 2 == 0 and off + 2 * nf <= AC
            return ar[:, off:off + 2 * nf].bitcast(F32)

        def bankap(b_, c0=0, w=512):
            pt = pA if b_ < 4 else pB
            base = (b_ % 4) * 512
            return pt[:, base + c0:base + c0 + w]

        def ph0_sync(sy, D):
            for dst, src in [(identf[:], identf_in), (bias_pp[:], bias_pp_in)]:
                sy.dma_start(dst, src).then_inc(D(), 16)
        t_ph0s = add("sync", ph0_sync, n_dma=2)

        def ph0_g(g, D):
            for dst, src in [
                    (ident16[:], identf_in), (b16(O_CC, S), cc_in),
                    (b16(O_SGN, S), sgn_in),
                    (ccq[:], cc_in[:, 0:TOK]), (sgnq[:], sgn_in[:, 0:TOK]),
                    (brv16[:], brv_in), (bro16[:], bro_in), (brf16[:], brf_in),
                    (emat16[:], emat_in), (wr2_sb[:], w_in["wr2"]),
                    (rot16[:], rotm_in),
                    (b16(O_WKV, 8192), w_in["wkv"]),
                    (b16(O_WQD, 8192), w_in["wqd"]),
                    (b16(O_WRKH, 8192), w_in["wrk"][:, 0:8192])]:
                g.dma_start(dst, src).then_inc(D(), 16)
        t_ph0g = add("gpsimd", ph0_g, n_dma=14)

        def ph0_v2a(v):
            return v.tensor_copy(bias_pp[0:32, 67:68],
                                 bias_pp[0:32, BC_R2:BC_R2 + 1])

        def ph0_v2(v):
            return v.tensor_tensor(bias_pp[0:16, 67:68], bias_pp[0:16, 67:68],
                                   bias_pp[0:16, BC_G:BC_G + 1], OP.add)

        def ph0_v(v):
            v.memset(ones16[:], 1.0)
            v.memset(consts[:, 0:1], EPS)
            v.memset(consts[:, 1:2], -SHIFT)
            v.memset(consts[:, 2:3], SCALE)
            v.memset(consts[:, 3:4], -2.0)
            v.memset(consts[:, 4:5], 1.0)
            return v.memset(consts[:, 5:6], -1.0)
        t_consts = add("vector", ph0_v)
        add("vector", ph0_v2a, waits=[("dq0", 16), ("dq1", 16)])
        t_cb = add("vector", ph0_v2)

        def emit_ln(src_tile_fn, n_tiles, xin_toks, dst_fm_off, dst_fm_stride,
                    tokbuf_off, nbuf_tok, first_tr_waits=()):
            tok_agg = [None] * n_tiles
            tok_st = [None] * n_tiles
            tok_pre = [None] * n_tiles
            tok_sq = [None] * n_tiles
            tok_nrm = [None] * n_tiles
            tok_tr = [None] * n_tiles
            tok_ev = [None] * n_tiles
            for i in range(n_tiles + 2):
                if i >= 2:
                    j = i - 2
                    io = j % 2

                    def preA(v, io=io):
                        v.tensor_scalar(lnri[:, io:io + 1],
                                        lnag[:, 2 * io + 1:2 * io + 2],
                                        C_EPS, None, OP.add)
                        return v.tensor_scalar(lnnm[:, io:io + 1],
                                               lnag[:, 2 * io:2 * io + 1],
                                               C_NEG1, None, OP.mult)
                    add("vector", preA,
                        waits=[tok_agg[j], tok_sq[j - 2] if j >= 2 else None])

                    def preB(v, io=io):
                        return v.reciprocal(lnr2[:, io:io + 1], lnri[:, io:io + 1])
                    tok_pre[j] = add("vector", preB)

                    def sq(sc, io=io):
                        return sc.activation(lnr[:, io:io + 1], lnr2[:, io:io + 1],
                                             AF.Sqrt)
                    tok_sq[j] = add("scalar", sq,
                                    waits=[tok_pre[j],
                                           tok_nrm[j - 2] if j >= 2 else None])

                    tb = j % nbuf_tok

                    def nrm(v, j=j, io=io, tb=tb):
                        return v.tensor_scalar(
                            b16(tokbuf_off + tb * 2048, 2048), src_tile_fn(j),
                            lnnm[:, io:io + 1], lnr[:, io:io + 1],
                            OP.add, OP.mult)
                    tok_nrm[j] = add(
                        "vector", nrm,
                        waits=[tok_sq[j],
                               tok_ev[j - nbuf_tok] if j >= nbuf_tok else None])

                    def tr(t, tb=tb):
                        for fc in range(16):
                            b_ = fc % 8
                            reg = fc // 8
                            r = t.matmul(bankap(b_, reg * 128, 128),
                                         b16(tokbuf_off + tb * 2048 + fc * 128, 128),
                                         ident16[:], start=True, stop=True)
                        return r
                    tok_tr[j] = add("tensor", tr,
                                    waits=[tok_nrm[j],
                                           tok_ev[j - 1] if j >= 1 else None] +
                                    (list(first_tr_waits) if j == 0 else []))

                    def ev(sc, j=j):
                        for pi, pt in enumerate([pA, pB]):
                            dst = bass.AP(
                                ar, dst_fm_off + j * 128 +
                                pi * 4 * dst_fm_stride,
                                [[AC, 128], [dst_fm_stride, 4],
                                 [8 * dst_fm_stride, 2], [1, 128]])
                            src = bass.AP(pt, 0,
                                          [[2048, 128], [512, 4], [128, 2],
                                           [1, 128]])
                            r = sc.activation(dst, src, AF.Copy)
                        return r
                    tok_ev[j] = add("scalar", ev, waits=[tok_tr[j]])
                if 1 <= i <= n_tiles:
                    jj = i - 1
                    if jj <= 1:
                        def spacer(v):
                            jop(v)
                            jop(v)
                            return jop(v)
                        add("vector", spacer)

                    def agg(v, jj=jj):
                        return v.bn_aggr(lnag[:, 2 * (jj % 2):2 * (jj % 2) + 2],
                                         lnst[:, (jj % 2) * 24:(jj % 2) * 24 + 24])
                    tok_agg[jj] = add("vector", agg,
                                      waits=[tok_st[jj],
                                             tok_pre[jj - 2] if jj >= 2 else None])
                if i < n_tiles:
                    def stats(v, i=i):
                        src = src_tile_fn(i)
                        base = (i % 2) * 24
                        for j2 in range(4):
                            r = v.bn_stats(lnst[:, base + 6 * j2:base + 6 * j2 + 6],
                                           src[:, 512 * j2:512 * j2 + 512])
                        return r
                    tok_st[i] = add("vector", stats,
                                    waits=[xin_toks[i] if xin_toks else None,
                                           t_consts if i == 0 else None,
                                           tok_agg[i - 2] if i >= 2 else None])
            return tok_ev, tok_nrm

        NT1 = 16
        xf3 = [xf[0][:], xf[1][:], f32v(69632, 2048)]
        tok_xin = [None] * NT1
        for i in range(NT1):
            def xin(sy, D, i=i):
                sy.dma_start(xf3[i % 3], xc[i * 128:(i + 1) * 128, :]
                             ).then_inc(D(), 16)
            tok_xin[i] = add("sync", xin, n_dma=1)
        ev1_toks, nrm1_toks = emit_ln(lambda i: xf3[i % 3], NT1, tok_xin,
                                      O_XLNFM, 2048, O_XLT, 2,
                                      first_tr_waits=[t_ph0g, t_ph0s])
        ph1_sync_ops = [o for o in ops["sync"] if o[0] == "dma"][-NT1:]
        for i in range(3, NT1):
            ph1_sync_ops[i][2].append(nrm1_toks[i - 3])

        fence()

        if debug and debug >= 1:
            _dm = _os.environ.get("DBG1", "abc")

            def d1(sy, D):
                if "a" in _dm:
                    sy.dma_start(dbg["d_xlnfm"], b16(O_XLNFM, 32768)
                                 ).then_inc(D(), 16)
                if "b" in _dm:
                    sy.dma_start(dbg["d_xf0"], xf[0][:]).then_inc(D(), 16)
                if "c" in _dm:
                    sy.dma_start(dbg["d_xlt"], b16(O_XLT, 4096)
                                 ).then_inc(D(), 16)
            add("sync", d1, n_dma=len(_dm))


        if PH < 2:
            def ydummy(sy, D):
                for r_ in range(8):
                    sy.dma_start(y[r_ * 128:(r_ + 1) * 128, :], xf[0][:]
                                 ).then_inc(D(), 16)
            add("sync", ydummy, n_dma=8)
            ops["sync"].append(("w", None, [(f"dq{li}", 16 * ((dq_next[0] - 1 - li) // 58 + 1)) for li in range(min(58, dq_next[0]))], None))
            PH_OFF[0] = True
        bank_tok = [None] * 8
        tmp_tok = [None] * 4
        last_dma = {"kv": [], "qd": [], "kr": []}

        def proj_job(wcol_fn, rhs_off_fn, bias_ap, out_hbm, out_col, jid,
                     extra_waits=(), rope=False, rope_g=None, rope_oc=None):
            b_ = jid % 6
            tb = jid % 2
            rb = 2 + jid % 2

            def mm(t, b_=b_):
                for fc in range(16):
                    r = t.matmul(bankap(b_), b16(wcol_fn(fc), 128),
                                 b16(rhs_off_fn(fc), 512),
                                 start=(fc == 0), stop=(fc == 15))
                return r
            tok_mm = add("tensor", mm, waits=[bank_tok[b_]] + list(extra_waits))

            if not rope:
                def ev(v, b_=b_, tb=tb, bias_ap=bias_ap):
                    return v.tensor_scalar(b16(O_KRT + tb * 512, 512),
                                           bankap(b_), bias_ap, None, OP.add)
                tok_e = add("vector", ev,
                            waits=[tok_mm, tmp_tok[tb]] +
                            ([t_ph0s, t_ph0g] if jid == 0 else []))
                src_off = O_KRT + tb * 512
                bufslot = tb
                bank_tok[b_] = tok_e
            else:
                rbank = 6 + jid % 2

                def eva(v, b_=b_, tb=tb, bias_ap=bias_ap):
                    kt = b16(O_KRT + tb * 512, 512)
                    return v.tensor_scalar(kt, bankap(b_), bias_ap, None, OP.add)
                tok_a = add("vector", eva, waits=[tok_mm, tmp_tok[tb],
                                                  tmp_tok[rb]])

                def rmm(t, tb=tb, rbank=rbank):
                    return t.matmul(bankap(rbank), rot16[:],
                                    b16(O_KRT + tb * 512, 512),
                                    start=True, stop=True)
                tok_r = add("tensor", rmm, waits=[tok_a, bank_tok[rbank]])

                def evb1a(v, tb=tb, rb=rb, g=rope_g):
                    return v.tensor_tensor(b16(O_KRT + rb * 512, 512),
                                           b16(O_KRT + tb * 512, 512),
                                           b16(O_CC + g * 512, 512), OP.mult)
                add("vector", evb1a, waits=[tok_r])

                def evb1b(v, tb=tb, g=rope_g, rbank=rbank):
                    return v.tensor_tensor(b16(O_KRT + tb * 512, 512),
                                           bankap(rbank),
                                           b16(O_SGN + g * 512, 512), OP.mult)
                add("vector", evb1b)

                def evb2(v, tb=tb, rb=rb):
                    return v.tensor_tensor(b16(O_KRT + rb * 512, 512),
                                           b16(O_KRT + rb * 512, 512),
                                           b16(O_KRT + tb * 512, 512), OP.add)
                tok_e = add("vector", evb2)
                tmp_tok[tb] = tok_e
                src_off = O_KRT + rb * 512
                bufslot = rb
                bank_tok[b_] = tok_a
                bank_tok[rbank] = tok_e

            def dout(sy, D, src_off=src_off, out_col=out_col):
                sy.dma_start(out_hbm[:, out_col:out_col + 512],
                             b16(src_off, 512)).then_inc(D(), 16)
            tok_d = add("sync", dout, waits=[tok_e], n_dma=1)
            tmp_tok[bufslot] = tok_d
            return tok_mm, tok_d

        jid = 0
        for lc in range(4):
            for g in range(4):
                _, d_ = proj_job(lambda fc, lc=lc: O_WKV + fc * 512 + lc * 128,
                                 lambda fc, g=g: O_XLNFM + fc * 2048 + g * 512,
                                 bias_pp[:, BC_KV + lc:BC_KV + lc + 1],
                                 kv_hbm, lc * 2048 + g * 512, jid,
                                 extra_waits=[t_ph0g] if jid == 0 else ())
                last_dma["kv"].extend(d_)
                jid += 1
        for lc in range(4):
            for g in range(2):
                _, d_ = proj_job(lambda fc, lc=lc: O_WQD + fc * 512 + lc * 128,
                                 lambda fc, g=g: O_XLNFM + fc * 2048 + g * 512,
                                 bias_pp[:, BC_QD + lc:BC_QD + lc + 1],
                                 qd_hbm, lc * 1024 + g * 512, jid,
                                 extra_waits=[t_ph0g] if (lc == 0 and g == 0) else ())
                last_dma["qd"].extend(d_)
                jid += 1
        tok_wrk2 = None
        last_half0_mm = None
        for oc in range(8):
            if oc == 4:
                def wrk2(g_, D):
                    g_.dma_start(b16(O_WRKH, 8192), w_in["wrk"][:, 8192:16384]
                                 ).then_inc(D(), 16)
                tok_wrk2 = add("gpsimd", wrk2, waits=[last_half0_mm], n_dma=1)
            for g in range(4):
                m_, d_ = proj_job(
                    lambda fc, oc=oc: O_WRKH + (oc % 4) * 2048 + fc * 128,
                    lambda fc, g=g: O_XLNFM + fc * 2048 + g * 512,
                    bias_pp[:, BC_RK + oc:BC_RK + oc + 1],
                    kr_hbm, oc * 2048 + g * 512, jid,
                    extra_waits=([tok_wrk2] if oc >= 4 else []) +
                    ([t_ph0g] if (oc == 0 and g == 0) else []),
                    rope=True, rope_g=g, rope_oc=oc)
                last_dma["kr"].extend(d_)
                jid += 1
                if oc == 3 and g == 3:
                    last_half0_mm = m_

        fence()

        if debug and debug >= 2:
            def d2(sy, D):
                sy.dma_start(dbg["d_kv"], kv_hbm).then_inc(D(), 16)
                sy.dma_start(dbg["d_qd"], qd_hbm).then_inc(D(), 16)
                sy.dma_start(dbg["d_kr"], kr_hbm).then_inc(D(), 16)
            add("sync", d2, waits=[last_dma["kv"], last_dma["qd"],
                                   last_dma["kr"]], n_dma=3)

        if PH < 3 and not PH_OFF[0]:
            def ydummy3(sy, D):
                for r_ in range(8):
                    sy.dma_start(y[r_ * 128:(r_ + 1) * 128, :], xf[0][:]
                                 ).then_inc(D(), 16)
            add("sync", ydummy3, n_dma=8)
            ops["sync"].append(("w", None, [(f"dq{li}", 16 * ((dq_next[0] - 1 - li) // 58 + 1)) for li in range(min(58, dq_next[0]))], None))
            PH_OFF[0] = True
        def ldkv(sy, D):
            sy.dma_start(b16(O_KV, 8192), kv_hbm).then_inc(D(), 16)
            sy.dma_start(b16(O_QD, 4096), qd_hbm).then_inc(D(), 16)
        tok_ldkv = add("sync", ldkv,
                       waits=last_dma["kv"] + last_dma["qd"] + last_dma["kr"],
                       n_dma=2)

        bank_tok = [None] * 8
        prev = {"exB": None, "ev1": None, "ev2": None, "pv": [None, None]}
        tok_attn_done = None
        u_glob = 0

        for w in range(2):
            def ldku(g_, D):
                g_.dma_start(b16(O_UPS, 4096), w_in["wku"]).then_inc(D(), 16)
                g_.dma_start(b16(O_UPS + 4096, 4096), w_in["wqu"]
                             ).then_inc(D(), 16)
            tok_ldku = add("gpsimd", ldku,
                           waits=[tok_attn_done if w else t_ph0g], n_dma=2)

            wave_toks = []

            jb = 0
            last_k1_ev = None
            for hw in range(8):
                for g in range(4):
                    b_ = jb % 6
                    h_ = 8 * w + hw

                    def mm(t, hw=hw, g=g, b_=b_, h_=h_):
                        for lc in range(4):
                            r = t.matmul(bankap(b_)[0:64, :],
                                         b16(O_UPS + lc * 1024 + h_ * 64, 64),
                                         b16(O_KV + lc * 2048 + g * 512, 512),
                                         start=(lc == 0), stop=(lc == 3))
                        return r
                    tok_mm = add("tensor", mm,
                                 waits=[tok_ldku, bank_tok[b_],
                                        tok_ldkv if jb == 0 else None,
                                        tok_attn_done if (w and jb == 0) else None])

                    def ev(v, hw=hw, g=g, b_=b_, h_=h_):
                        return v.tensor_scalar(
                            ar[0:64, O_KW + hw * 2048 + g * 512:
                               O_KW + hw * 2048 + g * 512 + 512],
                            bankap(b_)[0:64, :],
                            bias_pp[0:64, BC_KU + h_:BC_KU + h_ + 1],
                            None, OP.add)
                    tok_e = add("vector", ev, waits=[tok_mm])
                    bank_tok[b_] = tok_e
                    wave_toks.append(tok_e)
                    last_k1_ev = tok_e
                    jb += 1

            for hw in range(8):
                for g in range(2):
                    b_ = jb % 6
                    h_ = 8 * w + hw

                    def mm(t, g=g, b_=b_, h_=h_):
                        for lc in range(4):
                            r = t.matmul(bankap(b_)[0:64, :],
                                         b16(O_UPS + 4096 + lc * 1024 + h_ * 64, 64),
                                         b16(O_QD + lc * 1024 + g * 512, 512),
                                         start=(lc == 0), stop=(lc == 3))
                        return r
                    tok_mm = add("tensor", mm, waits=[bank_tok[b_]])

                    def ev(v, hw=hw, g=g, b_=b_, h_=h_):
                        return v.tensor_scalar(
                            ar[0:64, O_QW + hw * 1024 + g * 512:
                               O_QW + hw * 1024 + g * 512 + 512],
                            bankap(b_)[0:64, :],
                            bias_pp[0:64, BC_QU + h_:BC_QU + h_ + 1],
                            None, OP.add)
                    tok_e = add("vector", ev, waits=[tok_mm])
                    bank_tok[b_] = tok_e
                    wave_toks.append(tok_e)
                    jb += 1

            def ldrq(g_, D):
                g_.dma_start(b16(O_UPS, 4096), w_in["wrq"]).then_inc(D(), 16)
            tok_ldrq = add("gpsimd", ldrq, waits=[last_k1_ev], n_dma=1)

            last_qr_ev = None
            qr_dma_toks = [None, None]
            qidx = 0
            for hw in range(8):
                for g in range(2):
                    b_ = jb % 6
                    h_ = 8 * w + hw

                    def mm(t, g=g, b_=b_, h_=h_):
                        for lc in range(4):
                            r = t.matmul(bankap(b_)[0:64, :],
                                         b16(O_UPS + lc * 1024 + h_ * 64, 64),
                                         b16(O_QD + lc * 1024 + g * 512, 512),
                                         start=(lc == 0), stop=(lc == 3))
                        return r
                    tok_mm = add("tensor", mm, waits=[tok_ldrq, bank_tok[b_]])
                    rbank = 6 + jb % 2

                    def eva(v, b_=b_, h_=h_):
                        kt = b16(O_QRT, 512)
                        return v.tensor_scalar(
                            kt[0:64, :], bankap(b_)[0:64, :],
                            bias_pp[0:64, BC_RQ + h_:BC_RQ + h_ + 1],
                            None, OP.add)
                    tok_a = add("vector", eva, waits=[tok_mm, last_qr_ev])

                    def rmm(t, rbank=rbank):
                        return t.matmul(bankap(rbank)[0:64, :],
                                        rot16[0:64, 0:64],
                                        b16(O_QRT, 512)[0:64, :],
                                        start=True, stop=True)
                    tok_r = add("tensor", rmm, waits=[tok_a, bank_tok[rbank]])

                    kslot = qidx % 2

                    def evb1a(v, g=g, kslot=kslot):
                        ko = b16(O_QRT + 512 + kslot * 512, 512)
                        return v.tensor_tensor(
                            ko[0:64, :], b16(O_QRT, 512)[0:64, :],
                            ccq[0:64, g * 512:g * 512 + 512], OP.mult)
                    add("vector", evb1a, waits=[tok_r, qr_dma_toks[kslot]])

                    def evb1b(v, g=g, rbank=rbank):
                        return v.tensor_tensor(
                            b16(O_QRT, 512)[0:64, :], bankap(rbank)[0:64, :],
                            sgnq[0:64, g * 512:g * 512 + 512], OP.mult)
                    add("vector", evb1b)

                    def evb2(v, kslot=kslot):
                        ko = b16(O_QRT + 512 + kslot * 512, 512)
                        return v.tensor_tensor(ko[0:64, :], ko[0:64, :],
                                               b16(O_QRT, 512)[0:64, :], OP.add)
                    tok_e = add("vector", evb2)
                    bank_tok[b_] = tok_a
                    bank_tok[rbank] = tok_e

                    def qmv(sy, D, hw=hw, g=g, kslot=kslot):
                        sy.dma_start(
                            ar[64:128, O_QW + hw * 1024 + g * 512:
                               O_QW + hw * 1024 + g * 512 + 512],
                            ar[0:64, O_QRT + 512 + kslot * 512:
                               O_QRT + 1024 + kslot * 512]
                        ).then_inc(D(), 16)
                    qr_dma_toks[kslot] = add("sync", qmv, waits=[tok_e], n_dma=1)
                    wave_toks.append(qr_dma_toks[kslot])
                    last_qr_ev = tok_e
                    qidx += 1
                    jb += 1

            def ldkr(sy, D, w=w):
                for hw in range(8):
                    h_ = 8 * w + hw
                    oc, h2 = h_ // 2, h_ % 2
                    src = kr_hbm[h2 * 64:h2 * 64 + 64, oc * 2048:(oc + 1) * 2048]
                    dst = ar[64:128, O_KW + hw * 2048:O_KW + (hw + 1) * 2048]
                    sy.dma_start(dst, src).then_inc(D(), 16)
            tok_ldkr = add("sync", ldkr,
                           waits=last_dma["kr"] +
                           [tok_attn_done if w else None], n_dma=8)
            wave_toks.append(tok_ldkr)

            def ldvu(g_, D):
                g_.dma_start(b16(O_UPS, 8192), w_in["wvu"]).then_inc(D(), 16)
            tok_ldvu = add("gpsimd", ldvu, waits=[last_qr_ev], n_dma=1)

            def vones(v):
                return v.memset(bass.AP(ar, O_VW + 128,
                                        [[AC, 128], [129, 128], [1, 1]]), 1.0)
            tok_vone = add("vector", vones,
                           waits=[tok_attn_done if w else None])
            wave_toks.append(tok_vone)

            for tc in range(16):
                for dg in range(2):
                    b_ = jb % 6

                    def mm(t, tc=tc, dg=dg, b_=b_, w=w):
                        for lc in range(4):
                            t.matmul(bankap(b_), b16(O_KV + lc * 2048 + tc * 128, 128),
                                     b16(O_UPS + lc * 2048 + w * 1024 + dg * 512, 512),
                                     start=(lc == 0), stop=False)
                        return t.matmul(bankap(b_), ones16[:],
                                        brv16[0:1, w * 1024 + dg * 512:
                                              w * 1024 + dg * 512 + 512],
                                        start=False, stop=True)
                    tok_mm = add("tensor", mm, waits=[tok_ldvu, bank_tok[b_]])

                    def ev(v, tc=tc, dg=dg, b_=b_):
                        dst = bass.AP(ar, O_VW + (tc * 8 + dg * 4) * 129,
                                      [[AC, 128], [129, 4], [1, 128]])
                        return v.tensor_copy(dst, bankap(b_))
                    tok_e = add("vector", ev, waits=[tok_mm, tok_vone])
                    bank_tok[b_] = tok_e
                    wave_toks.append(tok_e)
                    jb += 1

            for hw in range(8):
                for g in range(4):
                    u = u_glob
                    p = u % 2
                    ORD = [0, 1, 2, 3, 4, 5, 6, 7] if p == 0 else \
                          [4, 5, 6, 7, 0, 1, 2, 3]
                    pt_off = O_PT0 if p == 0 else O_PT1
                    pHA = pA if ORD[0] == 0 else pB
                    pHB = pB if ORD[0] == 0 else pA
                    ob = [ORD[0], ORD[1]]
                    tbk = [ORD[2], ORD[3]]

                    def st(t, hw=hw, g=g, ORD=ORD, u=u,
                           w_ev1=prev["ev1"], w_ev2=prev["ev2"],
                           w_exB=prev["exB"],
                           w_first=(wave_toks if (hw == 0 and g == 0) else None)):
                        if w_first:
                            for tk in _flat(w_first):
                                t.wait_ge(sems[tk[0]], tk[1])
                        if w_exB:
                            t.wait_ge(sems[w_exB[0]], w_exB[1])
                        toks_mid = {4: w_ev1, 6: w_ev2}
                        r = None
                        for c in range(16):
                            if c in toks_mid and toks_mid[c] is not None:
                                tk = toks_mid[c]
                                t.wait_ge(sems[tk[0]], tk[1])
                            r = t.matmul(
                                bankap(ORD[c % 8], (c // 8) * 256, 256),
                                b16(O_KW + hw * 2048 + c * 128, 128),
                                b16(O_QW + hw * 1024 + g * 256, 256),
                                start=True, stop=True)
                            if c == 11:
                                r.then_inc(sems["t"], 1)
                        return r
                    sn = SEM_OF["tensor"]
                    cnt[sn] += 1
                    tok_stA = (sn, cnt[sn])
                    tok_stB = add("tensor", st)

                    def exA(sc, pHA=pHA, pt_off=pt_off,
                            w_pv2=prev["pv"][u % 2]):
                        if w_pv2:
                            sc.wait_ge(sems[w_pv2[0]], w_pv2[1])
                        dst = bass.AP(ar, pt_off,
                                      [[AC, 128], [256, 4], [2048, 2], [1, 256]])
                        return sc.activation(dst, pHA[:, 0:2048], AF.Exp,
                                             bias=C_NSH, scale=C_SCL)
                    tok_exA = add("scalar", exA, waits=[tok_stA])

                    def exB(sc, pHB=pHB, pt_off=pt_off):
                        dst = bass.AP(ar, pt_off + 1024,
                                      [[AC, 128], [256, 4], [2048, 2], [1, 256]])
                        return sc.activation(dst, pHB[:, 0:2048], AF.Exp,
                                             bias=C_NSH, scale=C_SCL)
                    tok_exB = add("scalar", exB, waits=[tok_stB])

                    CA = [0, 1, 2, 3, 8, 9, 10, 11]
                    CB = [4, 5, 6, 7, 12, 13, 14, 15]

                    def pv(t, hw=hw, pt_off=pt_off, ob=ob, CA=CA, CB=CB,
                           tok_exB=tok_exB):
                        r = None
                        for qh in range(2):
                            for k_, c in enumerate(CA):
                                r = t.matmul(
                                    bankap(ob[qh], 0, 129),
                                    b16(pt_off + c * 256 + qh * 128, 128),
                                    b16(O_VW + (c * 8 + hw) * 129, 129),
                                    start=(k_ == 0), stop=False)
                        t.wait_ge(sems[tok_exB[0]], tok_exB[1])
                        for qh in range(2):
                            for k_, c in enumerate(CB):
                                r = t.matmul(
                                    bankap(ob[qh], 0, 129),
                                    b16(pt_off + c * 256 + qh * 128, 128),
                                    b16(O_VW + (c * 8 + hw) * 129, 129),
                                    start=False, stop=(k_ == 7))
                        return r
                    tok_pv = add("tensor", pv, waits=[tok_exA])

                    def ev1a(v, ob=ob):
                        v.reciprocal(zin[:, 0:1], bankap(ob[0], 128, 1))
                        return v.reciprocal(zin[:, 1:2], bankap(ob[1], 128, 1))
                    add("vector", ev1a, waits=[tok_pv])

                    def ev1b(v, ob=ob):
                        v.tensor_scalar(b16(O_O2SB, 128), bankap(ob[0], 0, 128),
                                        zin[:, 0:1], None, OP.mult)
                        return v.tensor_scalar(b16(O_O2SB + 128, 128),
                                               bankap(ob[1], 0, 128),
                                               zin[:, 1:2], None, OP.mult)
                    tok_ev1 = add("vector", ev1b)

                    def trp(t, tbk=tbk):
                        r = None
                        for qh in range(2):
                            r = t.matmul(bankap(tbk[qh], 0, 128),
                                         b16(O_O2SB + qh * 128, 128),
                                         ident16[:], start=True, stop=True)
                        return r
                    tok_trp = add("tensor", trp, waits=[tok_ev1])

                    def ev2(v, hw=hw, g=g, w=w, tbk=tbk):
                        r = None
                        for qh in range(2):
                            dst = b16(O_ATTN + (8 * w + hw) * 1024 + g * 256 +
                                      qh * 128, 128)
                            r = v.tensor_copy(dst, bankap(tbk[qh], 0, 128))
                        return r
                    tok_ev2 = add("vector", ev2, waits=[tok_trp])

                    prev["exB"] = tok_exB
                    prev["ev1"] = tok_ev1
                    prev["ev2"] = tok_ev2
                    prev["pv"][u % 2] = tok_pv
                    tok_attn_done = tok_ev2
                    u_glob += 1

        fence()

        if debug and debug >= 3:
            def d3(sy, D):
                sy.dma_start(dbg["d_kw"], b16(O_KW, 16384)).then_inc(D(), 16)
                sy.dma_start(dbg["d_vw"], b16(O_VW, 16512)).then_inc(D(), 16)
                sy.dma_start(dbg["d_qw"], b16(O_QW, 8192)).then_inc(D(), 16)
                sy.dma_start(dbg["d_attn"], b16(O_ATTN, 16384)
                             ).then_inc(D(), 16)
            add("sync", d3, n_dma=4)

        if PH < 5 and not PH_OFF[0]:
            def ydummy5(sy, D):
                for r_ in range(8):
                    sy.dma_start(y[r_ * 128:(r_ + 1) * 128, :], xf[0][:]
                                 ).then_inc(D(), 16)
            add("sync", ydummy5, n_dma=8)
            ops["sync"].append(("w", None, [(f"dq{li}", 16 * ((dq_next[0] - 1 - li) // 58 + 1)) for li in range(min(58, dq_next[0]))], None))
            PH_OFF[0] = True
        def ldwo(g_, D):
            g_.dma_start(b16(O_WOA, 16384), w_in["woa"]).then_inc(D(), 16)
            g_.dma_start(b16(O_WOB, 16384), w_in["wob"]).then_inc(D(), 16)
        tok_ldwo = add("gpsimd", ldwo, waits=[tok_attn_done], n_dma=2)

        bank_tok = [None] * 8
        tok_xo = [None] * 8
        for tt in range(8):
            def xo(sy, D, tt=tt):
                sy.dma_start(xf[tt % 2][:], xc[tt * 128:(tt + 1) * 128, :]
                             ).then_inc(D(), 16)
            tok_xo[tt] = add("sync", xo, n_dma=1)

        wo_ev = {}
        for tt in range(8):
            for og in range(4):
                b_ = (tt % 2) * 4 + og
                woff = O_WOA if og < 2 else O_WOB

                def mm(t, tt=tt, og=og, b_=b_, woff=woff):
                    for fc in range(16):
                        t.matmul(bankap(b_),
                                 b16(O_ATTN + fc * 1024 + tt * 128, 128),
                                 b16(woff + fc * 1024 + (og % 2) * 512, 512),
                                 start=(fc == 0), stop=False)
                    return t.matmul(bankap(b_), ones16[:],
                                    bro16[0:1, og * 512:og * 512 + 512],
                                    start=False, stop=True)
                tok_mm = add("tensor", mm, waits=[tok_ldwo, bank_tok[b_]])

                def ev(v, tt=tt, og=og, b_=b_):
                    return v.tensor_tensor(
                        f32v(O_X2, 16384)[:, tt * 2048 + og * 512:
                                          tt * 2048 + og * 512 + 512],
                        bankap(b_), xf[tt % 2][:, og * 512:og * 512 + 512],
                        OP.add)
                tok_e = add("vector", ev, waits=[tok_mm, tok_xo[tt]])
                bank_tok[b_] = tok_e
                wo_ev[(tt, og)] = tok_e
        if not PH_OFF[0]:
            ph5_sync = [o for o in ops["sync"] if o[0] == "dma"][-8:]
            for tt in range(2, 8):
                ph5_sync[tt][2].append(wo_ev[(tt - 2, 3)])

        fence()
        if debug and debug >= 4:
            def d4(sy, D):
                sy.dma_start(dbg["d_x2"], b16(O_X2, 32768).bitcast(F32) if False
                             else f32v(O_X2, 16384)).then_inc(D(), 16)
            add("sync", d4, n_dma=1)

        if PH < 6 and not PH_OFF[0]:
            def ydummy6(sy, D):
                for r_ in range(8):
                    sy.dma_start(y[r_ * 128:(r_ + 1) * 128, :], xf[0][:]
                                 ).then_inc(D(), 16)
            add("sync", ydummy6, n_dma=8)
            ops["sync"].append(("w", None, [(f"dq{li}", 16 * ((dq_next[0] - 1 - li) // 58 + 1)) for li in range(min(58, dq_next[0]))], None))
            PH_OFF[0] = True
        NT2 = 8
        ev2_toks, _ = emit_ln(
            lambda i: f32v(O_X2, 16384)[:, i * 2048:(i + 1) * 2048],
            NT2, None, O_X2LN, 1024, O_TOKB, 1)

        fence()
        if debug and debug >= 5:
            def d5(sy, D):
                sy.dma_start(dbg["d_x2ln"], b16(O_X2LN, 16384)
                             ).then_inc(D(), 16)
            add("sync", d5, n_dma=1)

        if PH < 7 and not PH_OFF[0]:
            def ydummy7(sy, D):
                for r_ in range(8):
                    sy.dma_start(y[r_ * 128:(r_ + 1) * 128, :], xf[0][:]
                                 ).then_inc(D(), 16)
            add("sync", ydummy7, n_dma=8)
            ops["sync"].append(("w", None, [(f"dq{li}", 16 * ((dq_next[0] - 1 - li) // 58 + 1)) for li in range(min(58, dq_next[0]))], None))
            PH_OFF[0] = True
        def ldffn1(g_, D):
            g_.dma_start(b16(O_WA, 4096), w_in["wa"]).then_inc(D(), 16)
            g_.dma_start(b16(O_WB, 4096), w_in["wb"]).then_inc(D(), 16)
            g_.dma_start(b16(O_WR1, 1024), w_in["wr1"]).then_inc(D(), 16)
        tok_ldffn1 = add("gpsimd", ldffn1, n_dma=3)

        bank_tok = [None] * 8
        tok_hev = []
        for g in range(2):
            def mm(t, g=g):
                for fc in range(16):
                    r = t.matmul(bankap(g)[0:64, :], b16(O_WR1 + fc * 64, 64),
                                 b16(O_X2LN + fc * 1024 + g * 512, 512),
                                 start=(fc == 0), stop=(fc == 15))
                return r
            tok_mm = add("tensor", mm, waits=[tok_ldffn1])

            def ev(sc, g=g):
                return sc.activation(
                    ar[0:64, O_HSB + g * 512:O_HSB + g * 512 + 512],
                    bankap(g)[0:64, :], AF.Silu,
                    bias=bias_pp[0:64, BC_R1:BC_R1 + 1])
            tok_hev.append(add("scalar", ev, waits=[tok_mm]))

        tok_lg = []
        for g in range(2):
            def mm(t, g=g):
                return t.matmul(bankap(2 + g)[0:32, :], wr2_sb[:],
                                ar[0:64, O_HSB + g * 512:O_HSB + g * 512 + 512],
                                start=True, stop=True)
            tok_mm = add("tensor", mm, waits=[tok_hev[g]])

            def ev(v, g=g):
                return v.tensor_scalar(
                    f32v(O_LGT, 1024)[0:32, g * 512:g * 512 + 512],
                    bankap(2 + g)[0:32, :],
                    bias_pp[0:32, 67:68], None, OP.add)
            tok_lg.append(add("vector", ev, waits=[tok_mm, t_cb]))

        tok_lgt = []
        for tt in range(8):
            def mm(t, tt=tt):
                return t.matmul(bankap(4 + tt % 4, 0, 16)[0:128, :] if False
                                else bankap(4 + tt % 4, (tt // 4) * 16, 16),
                                f32v(O_LGT, 1024)[0:16, tt * 128:(tt + 1) * 128],
                                identf[0:16, 0:16], start=True, stop=True)
            tok_mm = add("tensor", mm, waits=[tok_lg[0], tok_lg[1],
                                              bank_tok[4 + tt % 4]])

            def ev(v, tt=tt):
                return v.tensor_copy(lgt_tok[:, tt * 16:(tt + 1) * 16],
                                     bankap(4 + tt % 4, (tt // 4) * 16, 16))
            tok_e = add("vector", ev, waits=[tok_mm])
            bank_tok[4 + tt % 4] = tok_e
            tok_lgt.append(tok_e)

        tok_coft = []
        for tt in range(8):
            lg_sl = lgt_tok[:, tt * 16:(tt + 1) * 16]

            def v1a(v, lg_sl=lg_sl):
                return v.max(out=m8[:], in_=lg_sl)
            t_v1a = add("vector", v1a, waits=[tok_lgt[tt]])

            def v1(v):
                return v.tensor_tensor(tks[:, 0:1], m8[:, 1:2], m8[:, 0:1],
                                       OP.subtract)
            t_v1 = add("vector", v1, waits=[t_v1a])

            def s1(sc):
                return sc.activation(tks[:, 1:2], tks[:, 0:1], AF.Sigmoid)
            t_s1 = add("scalar", s1, waits=[t_v1])

            def v2a(v, lg_sl=lg_sl):
                v.tensor_scalar(tks[:, 2:3], tks[:, 1:2], C_NEG2, C_ONE,
                                OP.mult, OP.add)
                v.tensor_scalar(msk[:, 0:16], lg_sl, m8[:, 1:2], None, OP.is_gt)
                return v.tensor_scalar(msk[:, 16:32], lg_sl, m8[:, 2:3],
                                       None, OP.is_gt)
            add("vector", v2a, waits=[t_s1])

            def v2b(v):
                v.tensor_scalar(msk[:, 32:48], msk[:, 0:16], tks[:, 2:3],
                                None, OP.mult)
                return v.tensor_scalar(msk[:, 48:64], msk[:, 16:32], tks[:, 1:2],
                                       None, OP.mult)
            add("vector", v2b)

            def v2c(v, tt=tt):
                return v.tensor_tensor(coef_tok[:, tt * 16:(tt + 1) * 16],
                                       msk[:, 32:48], msk[:, 48:64], OP.add)
            tok_coft.append(add("vector", v2c))

        tok_cfm = []
        for tt in range(8):
            def mm(t, tt=tt):
                return t.matmul(bankap(tt % 4, (tt // 4) * 128, 128)[0:16, :],
                                coef_tok[:, tt * 16:(tt + 1) * 16],
                                identf[:], start=True, stop=True)
            tok_mm = add("tensor", mm, waits=[tok_coft[tt], bank_tok[tt % 4]])

            def ev(v, tt=tt):
                return v.tensor_copy(
                    f32v(O_CFM, 1024)[0:16, tt * 128:(tt + 1) * 128],
                    bankap(tt % 4, (tt // 4) * 128, 128)[0:16, :])
            tok_e = add("vector", ev, waits=[tok_mm])
            bank_tok[tt % 4] = tok_e
            tok_cfm.append(tok_e)

        def cfcast(v):
            return v.tensor_copy(ar[0:16, O_CFMB:O_CFMB + 1024],
                                 f32v(O_CFM, 1024)[0:16, :])
        tok_cfb = add("vector", cfcast, waits=tok_cfm + tok_hev)

        tok_cexp = []
        for erc in range(2):
            for g in range(2):
                b_ = 4 + (erc * 2 + g) % 4

                def mm(t, erc=erc, g=g, b_=b_):
                    return t.matmul(bankap(b_),
                                    emat16[:, erc * 128:(erc + 1) * 128],
                                    ar[0:16, O_CFMB + g * 512:O_CFMB + g * 512 + 512],
                                    start=True, stop=True)
                tok_mm = add("tensor", mm, waits=[tok_cfb, bank_tok[b_]])

                def ev(v, erc=erc, g=g, b_=b_):
                    return v.tensor_copy(
                        b16(O_CEXP + erc * 1024 + g * 512, 512), bankap(b_))
                tok_e = add("vector", ev, waits=[tok_mm])
                bank_tok[b_] = tok_e
                tok_cexp.append(tok_e)

        tok_wxa = []
        for erc in range(2):
            for g in range(2):
                b_ = (erc * 2 + g) % 4

                def mm(t, erc=erc, g=g, b_=b_):
                    for fc in range(16):
                        r = t.matmul(bankap(b_),
                                     b16(O_WA + fc * 256 + erc * 128, 128),
                                     b16(O_X2LN + fc * 1024 + g * 512, 512),
                                     start=(fc == 0), stop=(fc == 15))
                    return r
                tok_mm = add("tensor", mm, waits=[tok_ldffn1, bank_tok[b_]])

                def ev(v, erc=erc, g=g, b_=b_):
                    return v.tensor_copy(b16(O_XA + erc * 1024 + g * 512, 512),
                                         bankap(b_))
                tok_e = add("vector", ev, waits=[tok_mm])
                bank_tok[b_] = tok_e

                def wx(v, erc=erc, g=g):
                    off = erc * 1024 + g * 512
                    return v.tensor_tensor(b16(O_WXA + off, 512),
                                           b16(O_XA + off, 512),
                                           b16(O_CEXP + off, 512), OP.mult)
                tok_wxa.append(add("vector", wx,
                                   waits=[tok_cexp[erc * 2 + g]]))

        fence()
        if debug and debug >= 6:
            def d6(sy, D):
                sy.dma_start(dbg["d_lgt"], lgt_tok[:]).then_inc(D(), 16)
                sy.dma_start(dbg["d_coef"], coef_tok[:]).then_inc(D(), 16)
                sy.dma_start(dbg["d_xa"], b16(O_XA, 2048)).then_inc(D(), 16)
                sy.dma_start(dbg["d_wxa"], b16(O_WXA, 2048)).then_inc(D(), 16)
                sy.dma_start(dbg["d_h"], b16(O_HSB, 1024)).then_inc(D(), 16)
            add("sync", d6, n_dma=5)

        if PH < 8 and not PH_OFF[0]:
            def ydummy8(sy, D):
                for r_ in range(8):
                    sy.dma_start(y[r_ * 128:(r_ + 1) * 128, :], xf[0][:]
                                 ).then_inc(D(), 16)
            add("sync", ydummy8, n_dma=8)
            ops["sync"].append(("w", None, [(f"dq{li}", 16 * ((dq_next[0] - 1 - li) // 58 + 1)) for li in range(min(58, dq_next[0]))], None))
            PH_OFF[0] = True
        def ldwf(g_, D):
            g_.dma_start(b16(O_WFFN, 16384), w_in["wfa"]).then_inc(D(), 16)
        tok_ldwf = add("gpsimd", ldwf, n_dma=1)

        bank_tok = [None] * 8
        out_tok = [None] * 8
        tok_wfb = None
        last_oga_mm = None
        n_ydma = 0
        for og in range(4):
            if og == 2:
                def ldwfb(g_, D):
                    g_.dma_start(b16(O_WFFN, 16384), w_in["wfb"]
                                 ).then_inc(D(), 16)
                tok_wfb = add("gpsimd", ldwfb, waits=[last_oga_mm], n_dma=1)
            for tt in range(8):
                b_ = (og * 8 + tt) % 8
                ob_ = (og * 8 + tt) % 8

                def mm(t, og=og, tt=tt, b_=b_):
                    for fc in range(16):
                        t.matmul(bankap(b_),
                                 b16(O_X2LN + fc * 1024 + tt * 128, 128),
                                 b16(O_WFFN + fc * 1024 + (og % 2) * 512, 512),
                                 start=(fc == 0), stop=False)
                    for erc in range(2):
                        t.matmul(bankap(b_),
                                 b16(O_WXA + erc * 1024 + tt * 128, 128),
                                 b16(O_WB + erc * 2048 + og * 512, 512),
                                 start=False, stop=False)
                    return t.matmul(bankap(b_), ones16[:],
                                    brf16[0:1, og * 512:og * 512 + 512],
                                    start=False, stop=True)
                tok_mm = add("tensor", mm,
                             waits=[tok_ldwf if og < 2 else tok_wfb,
                                    bank_tok[b_]] +
                             (tok_wxa if (og == 0 and tt == 0) else []))
                if og == 1 and tt == 7:
                    last_oga_mm = tok_mm

                def ev(v, og=og, tt=tt, b_=b_, ob_=ob_):
                    return v.tensor_tensor(
                        xf[ob_ // 4][:, (ob_ % 4) * 512:(ob_ % 4) * 512 + 512],
                        bankap(b_),
                        f32v(O_X2, 16384)[:, tt * 2048 + og * 512:
                                          tt * 2048 + og * 512 + 512],
                        OP.add)
                tok_e = add("vector", ev, waits=[tok_mm, out_tok[ob_]])
                bank_tok[b_] = tok_e

                def dy(sy, D, og=og, tt=tt, ob_=ob_):
                    sy.dma_start(y[tt * 128:(tt + 1) * 128,
                                   og * 512:og * 512 + 512],
                                 xf[ob_ // 4][:, (ob_ % 4) * 512:
                                              (ob_ % 4) * 512 + 512]
                                 ).then_inc(D(), 16)
                tok_d = add("sync", dy, waits=[tok_e], n_dma=1)
                out_tok[ob_] = tok_d
                n_ydma += 1

        ops["sync"].append(("w", None,
                            [(f"dq{li}", 16 * ((dq_next[0] - 1 - li) // 58 + 1))
                             for li in range(min(58, dq_next[0]))] +
                            [(f"dq{58 + li}", 16) for li in range(dq_next[1])],
                            None))

        for eng in ENGS:
            deco = getattr(blk, eng)
            myops = ops[eng]

            def runner(hh, myops=myops, eng=eng):
                sn = SEM_OF.get(eng)
                for opi, (kind, fn, waits, plan) in enumerate(myops):
                    if any(w is None or not isinstance(w, tuple) for w in waits):
                        raise RuntimeError(f"bad waits {eng} op{opi}: {waits}")
                    for wsn, val in waits:
                        hh.wait_ge(sems[wsn], val)
                    if kind == "cmp":
                        if plan is not None and plan > 0:
                            hh.wait_ge(sems[sn], plan)
                        instr = fn(hh)
                        instr.then_inc(sems[sn], 1)
                    elif kind == "dma":
                        it = iter(plan)

                        def D(hh=hh, it=it):
                            lane, frm = next(it)
                            if frm > 0:
                                hh.wait_ge(sems[lane], frm)
                            return sems[lane]
                        fn(hh, D)
            deco(runner)

    return nc, dbg


def _chunk128(a):
    r, c = a.shape
    assert r % 128 == 0
    return np.ascontiguousarray(
        a.reshape(r // 128, 128, c).transpose(1, 0, 2).reshape(128, -1))


def _prep_shared(i):
    f = np.float32
    out = {}
    out["identf"] = np.eye(128, dtype=f)
    bp = np.zeros((128, 68), f)
    bp[:, BC_KV:BC_KV + 4] = i["L0_attn_bkv_d"].reshape(4, 128).T
    bp[:, BC_QD:BC_QD + 4] = i["L0_attn_bq_d"].reshape(4, 128).T
    bp[0:64, BC_KU:BC_KU + 16] = i["L0_attn_bk_u"].reshape(16, 64).T
    bp[0:64, BC_QU:BC_QU + 16] = i["L0_attn_bq_u"].reshape(16, 64).T
    bp[:, BC_RK:BC_RK + 8] = i["L0_attn_brk"].reshape(8, 128).T
    bp[0:64, BC_RQ:BC_RQ + 16] = i["L0_attn_brq"].reshape(16, 64).T
    bp[0:64, BC_R1] = i["L0_ffn_rb1"]
    bp[0:32, BC_R2] = i["L0_ffn_rb2"]
    bp[0:16, BC_G] = i["L0_ffn_gates"]
    out["bias_pp"] = bp
    out["brv"] = i["L0_attn_bv_u"].reshape(1, E).astype(f)
    out["bro"] = i["L0_attn_bo"].reshape(1, E).astype(f)
    out["brf"] = i["L0_ffn_bias"].reshape(1, E).astype(f)
    em = np.zeros((16, 256), f)
    for e_ in range(16):
        em[e_, e_ * 16:(e_ + 1) * 16] = 1.0
    out["emat"] = em
    rotm = np.zeros((128, 128), f)
    for blk0 in (0, 64):
        for d in range(32):
            rotm[blk0 + d + 32, blk0 + d] = -1.0
            rotm[blk0 + d, blk0 + d + 32] = 1.0
    out["rotm"] = rotm
    out["wkv"] = _chunk128(i["L0_attn_Wkv_d"].T.astype(f))
    out["wqd"] = _chunk128(i["L0_attn_Wq_d"].T.astype(f))
    wrkT = _chunk128(i["L0_attn_Wrk"].T.astype(f))
    wrk = np.zeros((128, 16384), f)
    for oc in range(8):
        for fc in range(16):
            wrk[:, oc * 2048 + fc * 128:oc * 2048 + (fc + 1) * 128] = \
                wrkT[:, fc * 1024 + oc * 128:fc * 1024 + (oc + 1) * 128]
    out["wrk"] = np.ascontiguousarray(wrk)
    out["wku"] = _chunk128(i["L0_attn_Wk_u"].T.astype(f))
    out["wqu"] = _chunk128(i["L0_attn_Wq_u"].T.astype(f))
    out["wvu"] = _chunk128(i["L0_attn_Wv_u"].T.astype(f))
    out["wrq"] = _chunk128(i["L0_attn_Wrq"].T.astype(f))
    woT = _chunk128(i["L0_attn_Wo"].T.astype(f))
    out["woa"] = np.ascontiguousarray(
        woT.reshape(128, 16, 2048)[:, :, 0:1024].reshape(128, -1))
    out["wob"] = np.ascontiguousarray(
        woT.reshape(128, 16, 2048)[:, :, 1024:2048].reshape(128, -1))
    wfT = _chunk128(i["L0_ffn_W"].T.astype(f))
    out["wfa"] = np.ascontiguousarray(
        wfT.reshape(128, 16, 2048)[:, :, 0:1024].reshape(128, -1))
    out["wfb"] = np.ascontiguousarray(
        wfT.reshape(128, 16, 2048)[:, :, 1024:2048].reshape(128, -1))
    out["wa"] = _chunk128(np.ascontiguousarray(
        i["L0_ffn_A"].reshape(256, E).T.astype(f)))
    out["wb"] = _chunk128(np.ascontiguousarray(
        i["L0_ffn_B"].transpose(0, 2, 1).reshape(256, E).astype(f)))
    out["wr1"] = _chunk128(i["L0_ffn_rw1"].T.astype(f))
    out["wr2"] = np.ascontiguousarray(i["L0_ffn_rw2"].T.astype(f))
    return out


def _rope_tables(pos):
    inv = (1.0 / 10000.0 ** (np.arange(0, R, 2, dtype=np.float64) / R)
           ).astype(np.float32)
    t = pos.astype(np.float32)
    fr = np.outer(t, inv)
    emb = np.concatenate([fr, fr], axis=-1)
    cos = np.cos(emb).astype(np.float32)
    sin = np.sin(emb).astype(np.float32)
    cc = np.zeros((128, len(pos)), np.float32)
    sg = np.zeros((128, len(pos)), np.float32)
    for p in range(128):
        d = p % 64
        cc[p] = cos[:, d]
        sg[p] = sin[:, d]
    return cc, sg


def kernel(**inputs):
    shared = _prep_shared(inputs)
    x = np.asarray(inputs["L0_attn_x"], dtype=np.float32)
    nc, _ = build(debug=False)
    in_maps = []
    for c in range(8):
        b_, th = c // 2, c % 2
        if th == 0:
            xcore = np.ascontiguousarray(x[b_])
            pos = np.arange(S)
        else:
            xcore = np.ascontiguousarray(
                np.concatenate([x[b_, TOK:], x[b_, :TOK]], axis=0))
            pos = np.concatenate([np.arange(TOK, S), np.arange(TOK)])
        cc, sg = _rope_tables(pos)
        m = dict(shared)
        m["xc"] = xcore
        m["cc"] = cc
        m["sgn"] = sg
        in_maps.append(m)
    res = run_bass_kernel_spmd(nc, in_maps, core_ids=list(range(8)))
    out = np.empty((B, S, E), dtype=np.float32)
    for c in range(8):
        b_, th = c // 2, c % 2
        out[b_, th * TOK:(th + 1) * TOK, :] = res.results[c]["y"]
    return out


# version 03
# speedup vs baseline: 1.0627x; 1.0627x over previous
import math
import os as _os
import contextlib
import numpy as np
import concourse.bass as bass
import concourse.mybir as mybir
from concourse.bass_utils import run_bass_kernel_spmd

F32 = mybir.dt.float32
BF16 = mybir.dt.bfloat16
OP = mybir.AluOpType
AF = mybir.ActivationFunctionType

B, S, E = 4, 2048, 2048
H, LAT, D, R = 16, 512, 128, 64
TOK = 1024
SHIFT = 12.0
SCALE = 1.0 / math.sqrt(D)
EPS = 1e-5
AC = 88064

O_XLNFM, O_WKV, O_WQD, O_WRKH = 0, 32768, 40960, 49152
O_XLT, O_KRT = 57344, 61440
O_CC, O_SGN = 63488, 65536
O_KW, O_VW, O_QW = 0, 16384, 32896
O_PT0, O_PT1, O_UPS = 41088, 45184, 49280
O_KV, O_QD, O_ATTN = 57472, 65664, 69760
O_O2SB, O_QRT = 86144, 86400
O_WOA, O_X2, O_WOB = 0, 16384, 49152
O_X2LN, O_WFFN = 0, 49152
O_WA, O_WB, O_WR1, O_HSB = 65536, 69632, 73728, 74752
O_CEXP, O_XA, O_WXA = 75776, 77824, 79872
O_LGT, O_CFM = 81920, 83968
O_CFMB, O_TOKB = 73728, 86016

BC_KV, BC_QD, BC_KU, BC_QU, BC_RK, BC_RQ = 0, 4, 8, 24, 40, 48
BC_R1, BC_R2, BC_G = 64, 65, 66

ENGS = ("sync", "gpsimd", "tensor", "vector", "scalar")
SEM_OF = dict(sync="sd", gpsimd="sg", tensor="t", vector="v", scalar="s")


def build(debug=False):
    PH = int(_os.environ.get("BASS_PH", "99"))
    nc = bass.Bass("TRN2", target_bir_lowering=False)

    xc = nc.dram_tensor("xc", [S, E], F32, kind="ExternalInput").ap()
    cc_in = nc.dram_tensor("cc", [128, S], F32, kind="ExternalInput").ap()
    sgn_in = nc.dram_tensor("sgn", [128, S], F32, kind="ExternalInput").ap()
    identf_in = nc.dram_tensor("identf", [128, 128], F32, kind="ExternalInput").ap()
    bias_pp_in = nc.dram_tensor("bias_pp", [128, 68], F32, kind="ExternalInput").ap()
    brv_in = nc.dram_tensor("brv", [1, E], F32, kind="ExternalInput").ap()
    bro_in = nc.dram_tensor("bro", [1, E], F32, kind="ExternalInput").ap()
    brf_in = nc.dram_tensor("brf", [1, E], F32, kind="ExternalInput").ap()
    emat_in = nc.dram_tensor("emat", [16, 256], F32, kind="ExternalInput").ap()
    rotm_in = nc.dram_tensor("rotm", [128, 128], F32, kind="ExternalInput").ap()
    w_in = {}
    for name, shp in [("wkv", [128, 8192]), ("wqd", [128, 8192]),
                      ("wrk", [128, 16384]), ("wku", [128, 4096]),
                      ("wqu", [128, 4096]), ("wvu", [128, 8192]),
                      ("wrq", [128, 4096]), ("woa", [128, 16384]),
                      ("wob", [128, 16384]), ("wfa", [128, 16384]),
                      ("wfb", [128, 16384]), ("wa", [128, 4096]),
                      ("wb", [128, 4096]), ("wr1", [128, 1024]),
                      ("wr2", [64, 32])]:
        w_in[name] = nc.dram_tensor(name, shp, F32, kind="ExternalInput").ap()
    y = nc.dram_tensor("y", [TOK, E], F32, kind="ExternalOutput").ap()
    kv_hbm = nc.dram_tensor("kv_hbm", [128, 8192], BF16).ap()
    qd_hbm = nc.dram_tensor("qd_hbm", [128, 4096], BF16).ap()
    kr_hbm = nc.dram_tensor("kr_hbm", [128, 16384], BF16).ap()

    dbg = {}
    _dbg_shapes = {"d_xlnfm": ([128, 32768], BF16), "d_kv": ([128, 8192], BF16),
                   "d_qd": ([128, 4096], BF16), "d_kr": ([128, 16384], BF16),
                   "d_kw": ([128, 16384], BF16), "d_vw": ([128, 16512], BF16),
                   "d_qw": ([128, 8192], BF16), "d_attn": ([128, 16384], BF16),
                   "d_x2": ([128, 16384], F32), "d_x2ln": ([128, 16384], BF16),
                   "d_lgt": ([128, 128], F32), "d_coef": ([128, 128], F32),
                   "d_xa": ([128, 2048], BF16), "d_wxa": ([128, 2048], BF16),
                   "d_h": ([128, 1024], BF16), "d_xf0": ([128, 2048], F32),
                   "d_xlt": ([128, 4096], BF16), "d_lnx": ([128, 10], F32)}

    class _Dbg(dict):
        def __getitem__(self, k):
            if k not in self:
                shp, dt = _dbg_shapes[k]
                self[k] = nc.dram_tensor(k, shp, dt, kind="ExternalOutput").ap()
            return dict.__getitem__(self, k)
    dbg = _Dbg()
    if debug:
        lvl_names = {1: ["d_xlnfm", "d_xf0", "d_xlt"],
                     2: ["d_kv", "d_qd", "d_kr"],
                     3: ["d_kw", "d_vw", "d_qw", "d_attn"],
                     4: ["d_x2"], 5: ["d_x2ln"],
                     6: ["d_lgt", "d_coef", "d_xa", "d_wxa", "d_h"]}
        for lv in range(1, int(debug) + 1):
            for nm in lvl_names.get(lv, []):
                dbg[nm]

    ctx = contextlib.ExitStack()
    en = ctx.enter_context
    with ctx:
        ar = en(nc.sbuf_tensor("arena", [128, AC], BF16))
        ident16 = en(nc.sbuf_tensor("ident16", [128, 128], BF16))
        identf = en(nc.sbuf_tensor("identf_s", [128, 128], F32))
        ccq = en(nc.sbuf_tensor("ccq", [128, TOK], BF16))
        sgnq = en(nc.sbuf_tensor("sgnq", [128, TOK], BF16))
        brv16 = en(nc.sbuf_tensor("brv16", [1, E], BF16))
        bro16 = en(nc.sbuf_tensor("bro16", [1, E], BF16))
        brf16 = en(nc.sbuf_tensor("brf16", [1, E], BF16))
        bias_pp = en(nc.sbuf_tensor("bias_pp_s", [128, 68], F32))
        emat16 = en(nc.sbuf_tensor("emat16", [16, 256], BF16))
        ones16 = en(nc.sbuf_tensor("ones16", [1, 128], BF16))
        wr2_sb = en(nc.sbuf_tensor("wr2_sb", [64, 32], BF16))
        rot16 = en(nc.sbuf_tensor("rot16", [128, 128], BF16))
        consts = en(nc.sbuf_tensor("consts", [128, 8], F32))
        C_EPS = consts[:, 0:1]
        C_NSH = consts[:, 1:2]
        C_SCL = consts[:, 2:3]
        C_NEG2 = consts[:, 3:4]
        C_ONE = consts[:, 4:5]
        C_NEG1 = consts[:, 5:6]
        xf = [en(nc.sbuf_tensor(f"xf{i}", [128, E], F32)) for i in range(2)]
        lnst = en(nc.sbuf_tensor("lnst", [128, 48], F32))
        lnag = en(nc.sbuf_tensor("lnag", [128, 4], F32))
        lnnm = en(nc.sbuf_tensor("lnnm", [128, 2], F32))
        lnri = en(nc.sbuf_tensor("lnri", [128, 2], F32))
        lnr = en(nc.sbuf_tensor("lnr", [128, 2], F32))
        lnr2 = en(nc.sbuf_tensor("lnr2", [128, 2], F32))
        zin = en(nc.sbuf_tensor("zin", [128, 2], F32))
        lgt_tok = en(nc.sbuf_tensor("lgt_tok", [128, 128], F32))
        m8 = en(nc.sbuf_tensor("m8", [128, 8], F32))
        tks = en(nc.sbuf_tensor("tks", [128, 4], F32))
        msk = en(nc.sbuf_tensor("msk", [128, 64], F32))
        junk = en(nc.sbuf_tensor("junk", [128, 8], F32))
        _jc = [0]
        coef_tok = en(nc.sbuf_tensor("coef_tok", [128, 128], F32))
        pA = en(nc.psum_tensor("pA", [128, 2048], F32))
        pB = en(nc.psum_tensor("pB", [128, 2048], F32))
        sems = {k: en(nc.semaphore(k)) for k in ("t", "v", "s")}
        NDQ = 90
        for li in range(NDQ):
            sems[f"dq{li}"] = en(nc.semaphore(f"dq{li}"))
        blk = en(nc.Block("all"))

        ops = {e_: [] for e_ in ENGS}
        cnt = {s_: 0 for s_ in ("t", "v", "s")}
        dq_next = [0, 0]
        lastt = {e_: None for e_ in ENGS}

        PH_OFF = [False]

        def _flat(waits):
            out = []
            for w in waits:
                if w is None:
                    continue
                if isinstance(w, list):
                    out.extend(w)
                else:
                    out.append(w)
            return out

        def jop(v):
            _jc[0] = (_jc[0] + 1) % 8
            return v.memset(junk[:, _jc[0]:_jc[0] + 1], 0.0)

        def add(eng, fn, waits=(), n_dma=None):
            if PH_OFF[0]:
                return None
            ws = _flat(waits)
            if n_dma is not None:
                plan = []
                tok = []
                for _ in range(n_dma):
                    if eng == "gpsimd":
                        li = 58 + dq_next[1]
                        dq_next[1] += 1
                        assert li < 90, "out of gpsimd dma sems"
                        rnd = 0
                    else:
                        li = dq_next[0] % 58
                        rnd = dq_next[0] // 58
                        dq_next[0] += 1
                    lane = f"dq{li}"
                    plan.append((lane, 0))
                    tok.append((lane, 16 * (rnd + 1)))
                ops[eng].append(("dma", fn, ws, plan))
            else:
                sn = SEM_OF[eng]
                selfw = cnt[sn] if eng in ("vector", "scalar") else None
                cnt[sn] += 1
                tok = (sn, cnt[sn])
                ops[eng].append(("cmp", fn, ws, selfw))
            lastt[eng] = tok
            return tok

        def fence():
            toks = _flat([lastt[e_] for e_ in ("tensor", "vector", "scalar")])
            for e_ in ENGS:
                ops[e_].append(("w", None, list(toks), None))

        def b16(off, n):
            assert off + n <= AC
            return ar[:, off:off + n]

        def f32v(off, nf):
            assert off % 2 == 0 and off + 2 * nf <= AC
            return ar[:, off:off + 2 * nf].bitcast(F32)

        def bankap(b_, c0=0, w=512):
            pt = pA if b_ < 4 else pB
            base = (b_ % 4) * 512
            return pt[:, base + c0:base + c0 + w]

        def ph0_sync(sy, D):
            for dst, src in [(identf[:], identf_in), (bias_pp[:], bias_pp_in)]:
                sy.dma_start(dst, src).then_inc(D(), 16)
        t_ph0s = add("sync", ph0_sync, n_dma=2)

        def ph0_g(g, D):
            for dst, src in [
                    (ident16[:], identf_in), (b16(O_CC, S), cc_in),
                    (b16(O_SGN, S), sgn_in),
                    (ccq[:], cc_in[:, 0:TOK]), (sgnq[:], sgn_in[:, 0:TOK]),
                    (brv16[:], brv_in), (bro16[:], bro_in), (brf16[:], brf_in),
                    (emat16[:], emat_in), (wr2_sb[:], w_in["wr2"]),
                    (rot16[:], rotm_in),
                    (b16(O_WKV, 8192), w_in["wkv"]),
                    (b16(O_WQD, 8192), w_in["wqd"]),
                    (b16(O_WRKH, 8192), w_in["wrk"][:, 0:8192])]:
                g.dma_start(dst, src).then_inc(D(), 16)
        t_ph0g = add("gpsimd", ph0_g, n_dma=14)

        def ph0_v2a(v):
            return v.tensor_copy(bias_pp[0:32, 67:68],
                                 bias_pp[0:32, BC_R2:BC_R2 + 1])

        def ph0_v2(v):
            return v.tensor_tensor(bias_pp[0:16, 67:68], bias_pp[0:16, 67:68],
                                   bias_pp[0:16, BC_G:BC_G + 1], OP.add)

        def ph0_v(v):
            v.memset(ones16[:], 1.0)
            v.memset(consts[:, 0:1], EPS)
            v.memset(consts[:, 1:2], -SHIFT)
            v.memset(consts[:, 2:3], SCALE)
            v.memset(consts[:, 3:4], -2.0)
            v.memset(consts[:, 4:5], 1.0)
            return v.memset(consts[:, 5:6], -1.0)
        t_consts = add("vector", ph0_v)
        add("vector", ph0_v2a, waits=[("dq0", 16), ("dq1", 16)])
        t_cb = add("vector", ph0_v2)

        def emit_ln(src_tile_fn, n_tiles, xin_toks, dst_fm_off, dst_fm_stride,
                    tokbuf_off, nbuf_tok, first_tr_waits=()):
            tok_agg = [None] * n_tiles
            tok_st = [None] * n_tiles
            tok_pre = [None] * n_tiles
            tok_sq = [None] * n_tiles
            tok_nrm = [None] * n_tiles
            tok_tr = [None] * n_tiles
            tok_ev = [None] * n_tiles
            for i in range(n_tiles + 2):
                if i >= 2:
                    j = i - 2
                    io = j % 2

                    def preA(v, io=io):
                        v.tensor_scalar(lnri[:, io:io + 1],
                                        lnag[:, 2 * io + 1:2 * io + 2],
                                        C_EPS, None, OP.add)
                        return v.tensor_scalar(lnnm[:, io:io + 1],
                                               lnag[:, 2 * io:2 * io + 1],
                                               C_NEG1, None, OP.mult)
                    add("vector", preA,
                        waits=[tok_agg[j], tok_sq[j - 2] if j >= 2 else None])

                    def preB(v, io=io):
                        return v.reciprocal(lnr2[:, io:io + 1], lnri[:, io:io + 1])
                    tok_pre[j] = add("vector", preB)

                    def sq(sc, io=io):
                        return sc.activation(lnr[:, io:io + 1], lnr2[:, io:io + 1],
                                             AF.Sqrt)
                    tok_sq[j] = add("scalar", sq,
                                    waits=[tok_pre[j],
                                           tok_nrm[j - 2] if j >= 2 else None])

                    tb = j % nbuf_tok

                    def nrm(v, j=j, io=io, tb=tb):
                        return v.tensor_scalar(
                            b16(tokbuf_off + tb * 2048, 2048), src_tile_fn(j),
                            lnnm[:, io:io + 1], lnr[:, io:io + 1],
                            OP.add, OP.mult)
                    tok_nrm[j] = add(
                        "vector", nrm,
                        waits=[tok_sq[j],
                               tok_ev[j - nbuf_tok] if j >= nbuf_tok else None])

                    def tr(t, tb=tb):
                        for fc in range(16):
                            b_ = fc % 8
                            reg = fc // 8
                            r = t.matmul(bankap(b_, reg * 128, 128),
                                         b16(tokbuf_off + tb * 2048 + fc * 128, 128),
                                         ident16[:], start=True, stop=True)
                        return r
                    tok_tr[j] = add("tensor", tr,
                                    waits=[tok_nrm[j],
                                           tok_ev[j - 1] if j >= 1 else None] +
                                    (list(first_tr_waits) if j == 0 else []))

                    def ev(sc, j=j):
                        for pi, pt in enumerate([pA, pB]):
                            dst = bass.AP(
                                ar, dst_fm_off + j * 128 +
                                pi * 4 * dst_fm_stride,
                                [[AC, 128], [dst_fm_stride, 4],
                                 [8 * dst_fm_stride, 2], [1, 128]])
                            src = bass.AP(pt, 0,
                                          [[2048, 128], [512, 4], [128, 2],
                                           [1, 128]])
                            r = sc.activation(dst, src, AF.Copy)
                        return r
                    tok_ev[j] = add("scalar", ev, waits=[tok_tr[j]])
                if 1 <= i <= n_tiles:
                    jj = i - 1
                    if jj <= 1:
                        def spacer(v):
                            jop(v)
                            jop(v)
                            return jop(v)
                        add("vector", spacer)

                    def agg(v, jj=jj):
                        return v.bn_aggr(lnag[:, 2 * (jj % 2):2 * (jj % 2) + 2],
                                         lnst[:, (jj % 2) * 24:(jj % 2) * 24 + 24])
                    tok_agg[jj] = add("vector", agg,
                                      waits=[tok_st[jj],
                                             tok_pre[jj - 2] if jj >= 2 else None])
                if i < n_tiles:
                    def stats(v, i=i):
                        src = src_tile_fn(i)
                        base = (i % 2) * 24
                        for j2 in range(4):
                            r = v.bn_stats(lnst[:, base + 6 * j2:base + 6 * j2 + 6],
                                           src[:, 512 * j2:512 * j2 + 512])
                        return r
                    tok_st[i] = add("vector", stats,
                                    waits=[xin_toks[i] if xin_toks else None,
                                           t_consts if i == 0 else None,
                                           tok_agg[i - 2] if i >= 2 else None])
            return tok_ev, tok_nrm

        NT1 = 16
        xf3 = [xf[0][:], xf[1][:], f32v(69632, 2048)]
        tok_xin = [None] * NT1
        for i in range(NT1):
            def xin(sy, D, i=i):
                sy.dma_start(xf3[i % 3], xc[i * 128:(i + 1) * 128, :]
                             ).then_inc(D(), 16)
            tok_xin[i] = add("sync", xin, n_dma=1)
        ev1_toks, nrm1_toks = emit_ln(lambda i: xf3[i % 3], NT1, tok_xin,
                                      O_XLNFM, 2048, O_XLT, 2,
                                      first_tr_waits=[t_ph0g, t_ph0s])
        ph1_sync_ops = [o for o in ops["sync"] if o[0] == "dma"][-NT1:]
        for i in range(3, NT1):
            ph1_sync_ops[i][2].append(nrm1_toks[i - 3])

        fence()

        if debug and debug >= 1:
            _dm = _os.environ.get("DBG1", "abc")

            def d1(sy, D):
                if "a" in _dm:
                    sy.dma_start(dbg["d_xlnfm"], b16(O_XLNFM, 32768)
                                 ).then_inc(D(), 16)
                if "b" in _dm:
                    sy.dma_start(dbg["d_xf0"], xf[0][:]).then_inc(D(), 16)
                if "c" in _dm:
                    sy.dma_start(dbg["d_xlt"], b16(O_XLT, 4096)
                                 ).then_inc(D(), 16)
            add("sync", d1, n_dma=len(_dm))


        if PH < 2:
            def ydummy(sy, D):
                for r_ in range(8):
                    sy.dma_start(y[r_ * 128:(r_ + 1) * 128, :], xf[0][:]
                                 ).then_inc(D(), 16)
            add("sync", ydummy, n_dma=8)
            ops["sync"].append(("w", None, [(f"dq{li}", 16 * ((dq_next[0] - 1 - li) // 58 + 1)) for li in range(min(58, dq_next[0]))], None))
            PH_OFF[0] = True
        bank_tok = [None] * 8
        tmp_tok = [None] * 4
        last_dma = {"kv": [], "qd": [], "kr": []}

        def proj_job(wcol_fn, rhs_off_fn, bias_ap, out_hbm, out_col, jid,
                     extra_waits=(), rope=False, rope_g=None, rope_oc=None):
            b_ = jid % 6
            tb = jid % 2
            rb = 2 + jid % 2

            def mm(t, b_=b_):
                for fc in range(16):
                    r = t.matmul(bankap(b_), b16(wcol_fn(fc), 128),
                                 b16(rhs_off_fn(fc), 512),
                                 start=(fc == 0), stop=(fc == 15))
                return r
            tok_mm = add("tensor", mm, waits=[bank_tok[b_]] + list(extra_waits))

            if not rope:
                def ev(v, b_=b_, tb=tb, bias_ap=bias_ap):
                    return v.tensor_scalar(b16(O_KRT + tb * 512, 512),
                                           bankap(b_), bias_ap, None, OP.add)
                tok_e = add("vector", ev,
                            waits=[tok_mm, tmp_tok[tb]] +
                            ([t_ph0s, t_ph0g] if jid == 0 else []))
                src_off = O_KRT + tb * 512
                bufslot = tb
                bank_tok[b_] = tok_e
            else:
                rbank = 6 + jid % 2

                def eva(v, b_=b_, tb=tb, bias_ap=bias_ap):
                    kt = b16(O_KRT + tb * 512, 512)
                    return v.tensor_scalar(kt, bankap(b_), bias_ap, None, OP.add)
                tok_a = add("vector", eva, waits=[tok_mm, tmp_tok[tb],
                                                  tmp_tok[rb]])

                def rmm(t, tb=tb, rbank=rbank):
                    return t.matmul(bankap(rbank), rot16[:],
                                    b16(O_KRT + tb * 512, 512),
                                    start=True, stop=True)
                tok_r = add("tensor", rmm, waits=[tok_a, bank_tok[rbank]])

                def evb1a(v, tb=tb, rb=rb, g=rope_g):
                    return v.tensor_tensor(b16(O_KRT + rb * 512, 512),
                                           b16(O_KRT + tb * 512, 512),
                                           b16(O_CC + g * 512, 512), OP.mult)
                add("vector", evb1a, waits=[tok_r])

                def evb1b(v, tb=tb, g=rope_g, rbank=rbank):
                    return v.tensor_tensor(b16(O_KRT + tb * 512, 512),
                                           bankap(rbank),
                                           b16(O_SGN + g * 512, 512), OP.mult)
                add("vector", evb1b)

                def evb2(v, tb=tb, rb=rb):
                    return v.tensor_tensor(b16(O_KRT + rb * 512, 512),
                                           b16(O_KRT + rb * 512, 512),
                                           b16(O_KRT + tb * 512, 512), OP.add)
                tok_e = add("vector", evb2)
                tmp_tok[tb] = tok_e
                src_off = O_KRT + rb * 512
                bufslot = rb
                bank_tok[b_] = tok_a
                bank_tok[rbank] = tok_e

            def dout(sy, D, src_off=src_off, out_col=out_col):
                sy.dma_start(out_hbm[:, out_col:out_col + 512],
                             b16(src_off, 512)).then_inc(D(), 16)
            tok_d = add("sync", dout, waits=[tok_e], n_dma=1)
            tmp_tok[bufslot] = tok_d
            return tok_mm, tok_d

        jid = 0
        for lc in range(4):
            for g in range(4):
                _, d_ = proj_job(lambda fc, lc=lc: O_WKV + fc * 512 + lc * 128,
                                 lambda fc, g=g: O_XLNFM + fc * 2048 + g * 512,
                                 bias_pp[:, BC_KV + lc:BC_KV + lc + 1],
                                 kv_hbm, lc * 2048 + g * 512, jid,
                                 extra_waits=[t_ph0g] if jid == 0 else ())
                last_dma["kv"].extend(d_)
                jid += 1
        for lc in range(4):
            for g in range(2):
                _, d_ = proj_job(lambda fc, lc=lc: O_WQD + fc * 512 + lc * 128,
                                 lambda fc, g=g: O_XLNFM + fc * 2048 + g * 512,
                                 bias_pp[:, BC_QD + lc:BC_QD + lc + 1],
                                 qd_hbm, lc * 1024 + g * 512, jid,
                                 extra_waits=[t_ph0g] if (lc == 0 and g == 0) else ())
                last_dma["qd"].extend(d_)
                jid += 1
        tok_wrk2 = None
        last_half0_mm = None
        for oc in range(8):
            if oc == 4:
                def wrk2(g_, D):
                    g_.dma_start(b16(O_WRKH, 8192), w_in["wrk"][:, 8192:16384]
                                 ).then_inc(D(), 16)
                tok_wrk2 = add("gpsimd", wrk2, waits=[last_half0_mm], n_dma=1)
            for g in range(4):
                m_, d_ = proj_job(
                    lambda fc, oc=oc: O_WRKH + (oc % 4) * 2048 + fc * 128,
                    lambda fc, g=g: O_XLNFM + fc * 2048 + g * 512,
                    bias_pp[:, BC_RK + oc:BC_RK + oc + 1],
                    kr_hbm, oc * 2048 + g * 512, jid,
                    extra_waits=([tok_wrk2] if oc >= 4 else []) +
                    ([t_ph0g] if (oc == 0 and g == 0) else []),
                    rope=True, rope_g=g, rope_oc=oc)
                last_dma["kr"].extend(d_)
                jid += 1
                if oc == 3 and g == 3:
                    last_half0_mm = m_

        fence()

        if debug and debug >= 2:
            def d2(sy, D):
                sy.dma_start(dbg["d_kv"], kv_hbm).then_inc(D(), 16)
                sy.dma_start(dbg["d_qd"], qd_hbm).then_inc(D(), 16)
                sy.dma_start(dbg["d_kr"], kr_hbm).then_inc(D(), 16)
            add("sync", d2, waits=[last_dma["kv"], last_dma["qd"],
                                   last_dma["kr"]], n_dma=3)

        if PH < 3 and not PH_OFF[0]:
            def ydummy3(sy, D):
                for r_ in range(8):
                    sy.dma_start(y[r_ * 128:(r_ + 1) * 128, :], xf[0][:]
                                 ).then_inc(D(), 16)
            add("sync", ydummy3, n_dma=8)
            ops["sync"].append(("w", None, [(f"dq{li}", 16 * ((dq_next[0] - 1 - li) // 58 + 1)) for li in range(min(58, dq_next[0]))], None))
            PH_OFF[0] = True
        def ldkv(sy, D):
            sy.dma_start(b16(O_KV, 8192), kv_hbm).then_inc(D(), 16)
            sy.dma_start(b16(O_QD, 4096), qd_hbm).then_inc(D(), 16)
        tok_ldkv = add("sync", ldkv,
                       waits=last_dma["kv"] + last_dma["qd"] + last_dma["kr"],
                       n_dma=2)

        bank_tok = [None] * 8
        prev = {"exB1": None, "exB2": None, "ev1": None, "ev2": None,
                "pv": [None, None]}
        pend = None
        tok_attn_done = None
        u_glob = 0

        for w in range(2):
            def ldku(g_, D):
                g_.dma_start(b16(O_UPS, 4096), w_in["wku"]).then_inc(D(), 16)
                g_.dma_start(b16(O_UPS + 4096, 4096), w_in["wqu"]
                             ).then_inc(D(), 16)
            tok_ldku = add("gpsimd", ldku,
                           waits=[tok_attn_done if w else t_ph0g], n_dma=2)

            wave_toks = []

            jb = 0
            last_k1_ev = None
            for hw in range(8):
                for g in range(4):
                    b_ = jb % 6
                    h_ = 8 * w + hw

                    def mm(t, hw=hw, g=g, b_=b_, h_=h_):
                        for lc in range(4):
                            r = t.matmul(bankap(b_)[0:64, :],
                                         b16(O_UPS + lc * 1024 + h_ * 64, 64),
                                         b16(O_KV + lc * 2048 + g * 512, 512),
                                         start=(lc == 0), stop=(lc == 3))
                        return r
                    tok_mm = add("tensor", mm,
                                 waits=[tok_ldku, bank_tok[b_],
                                        tok_ldkv if jb == 0 else None,
                                        tok_attn_done if (w and jb == 0) else None])

                    def ev(v, hw=hw, g=g, b_=b_, h_=h_):
                        return v.tensor_scalar(
                            ar[0:64, O_KW + hw * 2048 + g * 512:
                               O_KW + hw * 2048 + g * 512 + 512],
                            bankap(b_)[0:64, :],
                            bias_pp[0:64, BC_KU + h_:BC_KU + h_ + 1],
                            None, OP.add)
                    tok_e = add("vector", ev, waits=[tok_mm])
                    bank_tok[b_] = tok_e
                    wave_toks.append(tok_e)
                    last_k1_ev = tok_e
                    jb += 1

            for hw in range(8):
                for g in range(2):
                    b_ = jb % 6
                    h_ = 8 * w + hw

                    def mm(t, g=g, b_=b_, h_=h_):
                        for lc in range(4):
                            r = t.matmul(bankap(b_)[0:64, :],
                                         b16(O_UPS + 4096 + lc * 1024 + h_ * 64, 64),
                                         b16(O_QD + lc * 1024 + g * 512, 512),
                                         start=(lc == 0), stop=(lc == 3))
                        return r
                    tok_mm = add("tensor", mm, waits=[bank_tok[b_]])

                    def ev(v, hw=hw, g=g, b_=b_, h_=h_):
                        return v.tensor_scalar(
                            ar[0:64, O_QW + hw * 1024 + g * 512:
                               O_QW + hw * 1024 + g * 512 + 512],
                            bankap(b_)[0:64, :],
                            bias_pp[0:64, BC_QU + h_:BC_QU + h_ + 1],
                            None, OP.add)
                    tok_e = add("vector", ev, waits=[tok_mm])
                    bank_tok[b_] = tok_e
                    wave_toks.append(tok_e)
                    jb += 1

            def ldrq(g_, D):
                g_.dma_start(b16(O_UPS, 4096), w_in["wrq"]).then_inc(D(), 16)
            tok_ldrq = add("gpsimd", ldrq, waits=[last_k1_ev], n_dma=1)

            last_qr_ev = None
            qr_dma_toks = [None, None]
            qidx = 0
            for hw in range(8):
                for g in range(2):
                    b_ = jb % 6
                    h_ = 8 * w + hw

                    def mm(t, g=g, b_=b_, h_=h_):
                        for lc in range(4):
                            r = t.matmul(bankap(b_)[0:64, :],
                                         b16(O_UPS + lc * 1024 + h_ * 64, 64),
                                         b16(O_QD + lc * 1024 + g * 512, 512),
                                         start=(lc == 0), stop=(lc == 3))
                        return r
                    tok_mm = add("tensor", mm, waits=[tok_ldrq, bank_tok[b_]])
                    rbank = 6 + jb % 2

                    def eva(v, b_=b_, h_=h_):
                        kt = b16(O_QRT, 512)
                        return v.tensor_scalar(
                            kt[0:64, :], bankap(b_)[0:64, :],
                            bias_pp[0:64, BC_RQ + h_:BC_RQ + h_ + 1],
                            None, OP.add)
                    tok_a = add("vector", eva, waits=[tok_mm, last_qr_ev])

                    def rmm(t, rbank=rbank):
                        return t.matmul(bankap(rbank)[0:64, :],
                                        rot16[0:64, 0:64],
                                        b16(O_QRT, 512)[0:64, :],
                                        start=True, stop=True)
                    tok_r = add("tensor", rmm, waits=[tok_a, bank_tok[rbank]])

                    kslot = qidx % 2

                    def evb1a(v, g=g, kslot=kslot):
                        ko = b16(O_QRT + 512 + kslot * 512, 512)
                        return v.tensor_tensor(
                            ko[0:64, :], b16(O_QRT, 512)[0:64, :],
                            ccq[0:64, g * 512:g * 512 + 512], OP.mult)
                    add("vector", evb1a, waits=[tok_r, qr_dma_toks[kslot]])

                    def evb1b(v, g=g, rbank=rbank):
                        return v.tensor_tensor(
                            b16(O_QRT, 512)[0:64, :], bankap(rbank)[0:64, :],
                            sgnq[0:64, g * 512:g * 512 + 512], OP.mult)
                    add("vector", evb1b)

                    def evb2(v, kslot=kslot):
                        ko = b16(O_QRT + 512 + kslot * 512, 512)
                        return v.tensor_tensor(ko[0:64, :], ko[0:64, :],
                                               b16(O_QRT, 512)[0:64, :], OP.add)
                    tok_e = add("vector", evb2)
                    bank_tok[b_] = tok_a
                    bank_tok[rbank] = tok_e

                    def qmv(sy, D, hw=hw, g=g, kslot=kslot):
                        sy.dma_start(
                            ar[64:128, O_QW + hw * 1024 + g * 512:
                               O_QW + hw * 1024 + g * 512 + 512],
                            ar[0:64, O_QRT + 512 + kslot * 512:
                               O_QRT + 1024 + kslot * 512]
                        ).then_inc(D(), 16)
                    qr_dma_toks[kslot] = add("sync", qmv, waits=[tok_e], n_dma=1)
                    wave_toks.append(qr_dma_toks[kslot])
                    last_qr_ev = tok_e
                    qidx += 1
                    jb += 1

            def ldkr(sy, D, w=w):
                for hw in range(8):
                    h_ = 8 * w + hw
                    oc, h2 = h_ // 2, h_ % 2
                    src = kr_hbm[h2 * 64:h2 * 64 + 64, oc * 2048:(oc + 1) * 2048]
                    dst = ar[64:128, O_KW + hw * 2048:O_KW + (hw + 1) * 2048]
                    sy.dma_start(dst, src).then_inc(D(), 16)
            tok_ldkr = add("sync", ldkr,
                           waits=last_dma["kr"] +
                           [tok_attn_done if w else None], n_dma=8)
            wave_toks.append(tok_ldkr)

            def ldvu(g_, D):
                g_.dma_start(b16(O_UPS, 8192), w_in["wvu"]).then_inc(D(), 16)
            tok_ldvu = add("gpsimd", ldvu, waits=[last_qr_ev], n_dma=1)

            def vones(v):
                return v.memset(bass.AP(ar, O_VW + 128,
                                        [[AC, 128], [129, 128], [1, 1]]), 1.0)
            tok_vone = add("vector", vones,
                           waits=[tok_attn_done if w else None])
            wave_toks.append(tok_vone)

            for tc in range(16):
                for dg in range(2):
                    b_ = jb % 6

                    def mm(t, tc=tc, dg=dg, b_=b_, w=w):
                        for lc in range(4):
                            t.matmul(bankap(b_), b16(O_KV + lc * 2048 + tc * 128, 128),
                                     b16(O_UPS + lc * 2048 + w * 1024 + dg * 512, 512),
                                     start=(lc == 0), stop=False)
                        return t.matmul(bankap(b_), ones16[:],
                                        brv16[0:1, w * 1024 + dg * 512:
                                              w * 1024 + dg * 512 + 512],
                                        start=False, stop=True)
                    tok_mm = add("tensor", mm, waits=[tok_ldvu, bank_tok[b_]])

                    def ev(v, tc=tc, dg=dg, b_=b_):
                        dst = bass.AP(ar, O_VW + (tc * 8 + dg * 4) * 129,
                                      [[AC, 128], [129, 4], [1, 128]])
                        return v.tensor_copy(dst, bankap(b_))
                    tok_e = add("vector", ev, waits=[tok_mm, tok_vone])
                    bank_tok[b_] = tok_e
                    wave_toks.append(tok_e)
                    jb += 1

            for hw in range(8):
                for g in range(4):
                    u = u_glob
                    p = u % 2
                    ORD = [0, 1, 2, 3, 4, 5, 6, 7] if p == 0 else \
                          [4, 5, 6, 7, 0, 1, 2, 3]
                    pt_off = O_PT0 if p == 0 else O_PT1
                    pHA = pA if ORD[0] == 0 else pB
                    pHB = pB if ORD[0] == 0 else pA
                    ob = [ORD[0], ORD[1]]
                    tbk = [ORD[2], ORD[3]]

                    def stH1(t, hw=hw, g=g, ORD=ORD,
                             w_exB2=prev["exB2"], w_ev1=prev["ev1"],
                             w_first=(wave_toks if (hw == 0 and g == 0)
                                      else None)):
                        if w_first:
                            for tk in _flat(w_first):
                                t.wait_ge(sems[tk[0]], tk[1])
                        r = None
                        for c in range(6):
                            if c == 2 and w_exB2:
                                t.wait_ge(sems[w_exB2[0]], w_exB2[1])
                            if c == 4 and w_ev1:
                                t.wait_ge(sems[w_ev1[0]], w_ev1[1])
                            r = t.matmul(
                                bankap(ORD[c % 8], (c // 8) * 256, 256),
                                b16(O_KW + hw * 2048 + c * 128, 128),
                                b16(O_QW + hw * 1024 + g * 256, 256),
                                start=True, stop=True)
                        return r
                    add("tensor", stH1, waits=[prev["exB1"]])

                    if pend is not None:
                        tok_trp_p = add("tensor", pend["trp"],
                                        waits=[pend["ev1"]])
                        tok_ev2_p = add("vector", pend["ev2"],
                                        waits=[tok_trp_p])
                        prev["ev2"] = tok_ev2_p
                        tok_attn_done = tok_ev2_p
                        pend = None

                    SEQ2 = [8, 9, 10, 11, 12, 13, 6, 7, 14, 15]
                    sn = SEM_OF["tensor"]
                    cnt[sn] += 1
                    tok_sA1 = (sn, cnt[sn])
                    cnt[sn] += 1
                    tok_sA2 = (sn, cnt[sn])
                    cnt[sn] += 1
                    tok_sB1 = (sn, cnt[sn])

                    def stH2(t, hw=hw, g=g, ORD=ORD, SEQ2=SEQ2,
                             w_ev2=prev["ev2"]):
                        r = None
                        for i_, c in enumerate(SEQ2):
                            if c == 6 and w_ev2:
                                t.wait_ge(sems[w_ev2[0]], w_ev2[1])
                            r = t.matmul(
                                bankap(ORD[c % 8], (c // 8) * 256, 256),
                                b16(O_KW + hw * 2048 + c * 128, 128),
                                b16(O_QW + hw * 1024 + g * 256, 256),
                                start=True, stop=True)
                            if i_ in (1, 3, 5):
                                r.then_inc(sems["t"], 1)
                        return r
                    tok_sB2 = add("tensor", stH2)

                    def mkexp(src_ap, dst_off, w_tok, w_pv2=None):
                        def ex(sc, src_ap=src_ap, dst_off=dst_off,
                               w_pv2=w_pv2):
                            if w_pv2:
                                sc.wait_ge(sems[w_pv2[0]], w_pv2[1])
                            dst = bass.AP(ar, dst_off,
                                          [[AC, 128], [256, 2], [2048, 2],
                                           [1, 256]])
                            return sc.activation(dst, src_ap, AF.Exp,
                                                 bias=C_NSH, scale=C_SCL)
                        return add("scalar", ex, waits=[w_tok])
                    tok_exA1 = mkexp(pHA[:, 0:1024], pt_off, tok_sA1,
                                     prev["pv"][u % 2])
                    tok_exA2 = mkexp(pHA[:, 1024:2048], pt_off + 512, tok_sA2)
                    tok_exB1 = mkexp(pHB[:, 0:1024], pt_off + 1024, tok_sB1)
                    tok_exB2 = mkexp(pHB[:, 1024:2048], pt_off + 1536, tok_sB2)

                    GROUPS = [([0, 1, 8, 9], None), ([2, 3, 10, 11], tok_exA2),
                              ([4, 5, 12, 13], tok_exB1),
                              ([6, 7, 14, 15], tok_exB2)]

                    def pv(t, hw=hw, pt_off=pt_off, ob=ob, GROUPS=GROUPS):
                        r = None
                        for gi, (chs, wt) in enumerate(GROUPS):
                            if wt is not None:
                                t.wait_ge(sems[wt[0]], wt[1])
                            for qh in range(2):
                                for ci, c in enumerate(chs):
                                    r = t.matmul(
                                        bankap(ob[qh], 0, 129),
                                        b16(pt_off + c * 256 + qh * 128, 128),
                                        b16(O_VW + (c * 8 + hw) * 129, 129),
                                        start=(gi == 0 and ci == 0),
                                        stop=(gi == 3 and ci == 3))
                        return r
                    tok_pv = add("tensor", pv, waits=[tok_exA1])

                    def ev1a(v, ob=ob):
                        v.reciprocal(zin[:, 0:1], bankap(ob[0], 128, 1))
                        return v.reciprocal(zin[:, 1:2], bankap(ob[1], 128, 1))
                    add("vector", ev1a, waits=[tok_pv])

                    def ev1b(v, ob=ob):
                        v.tensor_scalar(b16(O_O2SB, 128), bankap(ob[0], 0, 128),
                                        zin[:, 0:1], None, OP.mult)
                        return v.tensor_scalar(b16(O_O2SB + 128, 128),
                                               bankap(ob[1], 0, 128),
                                               zin[:, 1:2], None, OP.mult)
                    tok_ev1 = add("vector", ev1b)

                    def trp(t, tbk=tbk):
                        r = None
                        for qh in range(2):
                            r = t.matmul(bankap(tbk[qh], 0, 128),
                                         b16(O_O2SB + qh * 128, 128),
                                         ident16[:], start=True, stop=True)
                        return r

                    def ev2(v, hw=hw, g=g, w=w, tbk=tbk):
                        r = None
                        for qh in range(2):
                            dst = b16(O_ATTN + (8 * w + hw) * 1024 + g * 256 +
                                      qh * 128, 128)
                            r = v.tensor_copy(dst, bankap(tbk[qh], 0, 128))
                        return r
                    pend = {"trp": trp, "ev2": ev2, "ev1": tok_ev1}

                    prev["exB1"] = tok_exB1
                    prev["exB2"] = tok_exB2
                    prev["ev1"] = tok_ev1
                    prev["pv"][u % 2] = tok_pv
                    u_glob += 1

            if pend is not None:
                tok_trp_p = add("tensor", pend["trp"], waits=[pend["ev1"]])
                tok_ev2_p = add("vector", pend["ev2"], waits=[tok_trp_p])
                prev["ev2"] = tok_ev2_p
                tok_attn_done = tok_ev2_p
                pend = None

        fence()

        if debug and debug >= 3:
            def d3(sy, D):
                sy.dma_start(dbg["d_kw"], b16(O_KW, 16384)).then_inc(D(), 16)
                sy.dma_start(dbg["d_vw"], b16(O_VW, 16512)).then_inc(D(), 16)
                sy.dma_start(dbg["d_qw"], b16(O_QW, 8192)).then_inc(D(), 16)
                sy.dma_start(dbg["d_attn"], b16(O_ATTN, 16384)
                             ).then_inc(D(), 16)
            add("sync", d3, n_dma=4)

        if PH < 5 and not PH_OFF[0]:
            def ydummy5(sy, D):
                for r_ in range(8):
                    sy.dma_start(y[r_ * 128:(r_ + 1) * 128, :], xf[0][:]
                                 ).then_inc(D(), 16)
            add("sync", ydummy5, n_dma=8)
            ops["sync"].append(("w", None, [(f"dq{li}", 16 * ((dq_next[0] - 1 - li) // 58 + 1)) for li in range(min(58, dq_next[0]))], None))
            PH_OFF[0] = True
        def ldwo(g_, D):
            g_.dma_start(b16(O_WOA, 16384), w_in["woa"]).then_inc(D(), 16)
            g_.dma_start(b16(O_WOB, 16384), w_in["wob"]).then_inc(D(), 16)
        tok_ldwo = add("gpsimd", ldwo, waits=[tok_attn_done], n_dma=2)

        bank_tok = [None] * 8
        tok_xo = [None] * 8
        for tt in range(8):
            def xo(sy, D, tt=tt):
                sy.dma_start(xf[tt % 2][:], xc[tt * 128:(tt + 1) * 128, :]
                             ).then_inc(D(), 16)
            tok_xo[tt] = add("sync", xo, n_dma=1)

        wo_ev = {}
        for tt in range(8):
            for og in range(4):
                b_ = (tt % 2) * 4 + og
                woff = O_WOA if og < 2 else O_WOB

                def mm(t, tt=tt, og=og, b_=b_, woff=woff):
                    for fc in range(16):
                        t.matmul(bankap(b_),
                                 b16(O_ATTN + fc * 1024 + tt * 128, 128),
                                 b16(woff + fc * 1024 + (og % 2) * 512, 512),
                                 start=(fc == 0), stop=False)
                    return t.matmul(bankap(b_), ones16[:],
                                    bro16[0:1, og * 512:og * 512 + 512],
                                    start=False, stop=True)
                tok_mm = add("tensor", mm, waits=[tok_ldwo, bank_tok[b_]])

                def ev(v, tt=tt, og=og, b_=b_):
                    return v.tensor_tensor(
                        f32v(O_X2, 16384)[:, tt * 2048 + og * 512:
                                          tt * 2048 + og * 512 + 512],
                        bankap(b_), xf[tt % 2][:, og * 512:og * 512 + 512],
                        OP.add)
                tok_e = add("vector", ev, waits=[tok_mm, tok_xo[tt]])
                bank_tok[b_] = tok_e
                wo_ev[(tt, og)] = tok_e
        if not PH_OFF[0]:
            ph5_sync = [o for o in ops["sync"] if o[0] == "dma"][-8:]
            for tt in range(2, 8):
                ph5_sync[tt][2].append(wo_ev[(tt - 2, 3)])

        fence()
        if debug and debug >= 4:
            def d4(sy, D):
                sy.dma_start(dbg["d_x2"], b16(O_X2, 32768).bitcast(F32) if False
                             else f32v(O_X2, 16384)).then_inc(D(), 16)
            add("sync", d4, n_dma=1)

        if PH < 6 and not PH_OFF[0]:
            def ydummy6(sy, D):
                for r_ in range(8):
                    sy.dma_start(y[r_ * 128:(r_ + 1) * 128, :], xf[0][:]
                                 ).then_inc(D(), 16)
            add("sync", ydummy6, n_dma=8)
            ops["sync"].append(("w", None, [(f"dq{li}", 16 * ((dq_next[0] - 1 - li) // 58 + 1)) for li in range(min(58, dq_next[0]))], None))
            PH_OFF[0] = True
        NT2 = 8
        ev2_toks, _ = emit_ln(
            lambda i: f32v(O_X2, 16384)[:, i * 2048:(i + 1) * 2048],
            NT2, None, O_X2LN, 1024, O_TOKB, 1)

        fence()
        if debug and debug >= 5:
            def d5(sy, D):
                sy.dma_start(dbg["d_x2ln"], b16(O_X2LN, 16384)
                             ).then_inc(D(), 16)
            add("sync", d5, n_dma=1)

        if PH < 7 and not PH_OFF[0]:
            def ydummy7(sy, D):
                for r_ in range(8):
                    sy.dma_start(y[r_ * 128:(r_ + 1) * 128, :], xf[0][:]
                                 ).then_inc(D(), 16)
            add("sync", ydummy7, n_dma=8)
            ops["sync"].append(("w", None, [(f"dq{li}", 16 * ((dq_next[0] - 1 - li) // 58 + 1)) for li in range(min(58, dq_next[0]))], None))
            PH_OFF[0] = True
        def ldffn1(g_, D):
            g_.dma_start(b16(O_WA, 4096), w_in["wa"]).then_inc(D(), 16)
            g_.dma_start(b16(O_WB, 4096), w_in["wb"]).then_inc(D(), 16)
            g_.dma_start(b16(O_WR1, 1024), w_in["wr1"]).then_inc(D(), 16)
        tok_ldffn1 = add("gpsimd", ldffn1, n_dma=3)

        bank_tok = [None] * 8
        tok_hev = []
        for g in range(2):
            def mm(t, g=g):
                for fc in range(16):
                    r = t.matmul(bankap(g)[0:64, :], b16(O_WR1 + fc * 64, 64),
                                 b16(O_X2LN + fc * 1024 + g * 512, 512),
                                 start=(fc == 0), stop=(fc == 15))
                return r
            tok_mm = add("tensor", mm, waits=[tok_ldffn1])

            def ev(sc, g=g):
                return sc.activation(
                    ar[0:64, O_HSB + g * 512:O_HSB + g * 512 + 512],
                    bankap(g)[0:64, :], AF.Silu,
                    bias=bias_pp[0:64, BC_R1:BC_R1 + 1])
            tok_hev.append(add("scalar", ev, waits=[tok_mm]))

        tok_lg = []
        for g in range(2):
            def mm(t, g=g):
                return t.matmul(bankap(2 + g)[0:32, :], wr2_sb[:],
                                ar[0:64, O_HSB + g * 512:O_HSB + g * 512 + 512],
                                start=True, stop=True)
            tok_mm = add("tensor", mm, waits=[tok_hev[g]])

            def ev(v, g=g):
                return v.tensor_scalar(
                    f32v(O_LGT, 1024)[0:32, g * 512:g * 512 + 512],
                    bankap(2 + g)[0:32, :],
                    bias_pp[0:32, 67:68], None, OP.add)
            tok_lg.append(add("vector", ev, waits=[tok_mm, t_cb]))

        tok_lgt = []
        for tt in range(8):
            def mm(t, tt=tt):
                return t.matmul(bankap(4 + tt % 4, 0, 16)[0:128, :] if False
                                else bankap(4 + tt % 4, (tt // 4) * 16, 16),
                                f32v(O_LGT, 1024)[0:16, tt * 128:(tt + 1) * 128],
                                identf[0:16, 0:16], start=True, stop=True)
            tok_mm = add("tensor", mm, waits=[tok_lg[0], tok_lg[1],
                                              bank_tok[4 + tt % 4]])

            def ev(v, tt=tt):
                return v.tensor_copy(lgt_tok[:, tt * 16:(tt + 1) * 16],
                                     bankap(4 + tt % 4, (tt // 4) * 16, 16))
            tok_e = add("vector", ev, waits=[tok_mm])
            bank_tok[4 + tt % 4] = tok_e
            tok_lgt.append(tok_e)

        tok_coft = []
        for tt in range(8):
            lg_sl = lgt_tok[:, tt * 16:(tt + 1) * 16]

            def v1a(v, lg_sl=lg_sl):
                return v.max(out=m8[:], in_=lg_sl)
            t_v1a = add("vector", v1a, waits=[tok_lgt[tt]])

            def v1(v):
                return v.tensor_tensor(tks[:, 0:1], m8[:, 1:2], m8[:, 0:1],
                                       OP.subtract)
            t_v1 = add("vector", v1, waits=[t_v1a])

            def s1(sc):
                return sc.activation(tks[:, 1:2], tks[:, 0:1], AF.Sigmoid)
            t_s1 = add("scalar", s1, waits=[t_v1])

            def v2a(v, lg_sl=lg_sl):
                v.tensor_scalar(tks[:, 2:3], tks[:, 1:2], C_NEG2, C_ONE,
                                OP.mult, OP.add)
                v.tensor_scalar(msk[:, 0:16], lg_sl, m8[:, 1:2], None, OP.is_gt)
                return v.tensor_scalar(msk[:, 16:32], lg_sl, m8[:, 2:3],
                                       None, OP.is_gt)
            add("vector", v2a, waits=[t_s1])

            def v2b(v):
                v.tensor_scalar(msk[:, 32:48], msk[:, 0:16], tks[:, 2:3],
                                None, OP.mult)
                return v.tensor_scalar(msk[:, 48:64], msk[:, 16:32], tks[:, 1:2],
                                       None, OP.mult)
            add("vector", v2b)

            def v2c(v, tt=tt):
                return v.tensor_tensor(coef_tok[:, tt * 16:(tt + 1) * 16],
                                       msk[:, 32:48], msk[:, 48:64], OP.add)
            tok_coft.append(add("vector", v2c))

        tok_cfm = []
        for tt in range(8):
            def mm(t, tt=tt):
                return t.matmul(bankap(tt % 4, (tt // 4) * 128, 128)[0:16, :],
                                coef_tok[:, tt * 16:(tt + 1) * 16],
                                identf[:], start=True, stop=True)
            tok_mm = add("tensor", mm, waits=[tok_coft[tt], bank_tok[tt % 4]])

            def ev(v, tt=tt):
                return v.tensor_copy(
                    f32v(O_CFM, 1024)[0:16, tt * 128:(tt + 1) * 128],
                    bankap(tt % 4, (tt // 4) * 128, 128)[0:16, :])
            tok_e = add("vector", ev, waits=[tok_mm])
            bank_tok[tt % 4] = tok_e
            tok_cfm.append(tok_e)

        def cfcast(v):
            return v.tensor_copy(ar[0:16, O_CFMB:O_CFMB + 1024],
                                 f32v(O_CFM, 1024)[0:16, :])
        tok_cfb = add("vector", cfcast, waits=tok_cfm + tok_hev)

        tok_cexp = []
        for erc in range(2):
            for g in range(2):
                b_ = 4 + (erc * 2 + g) % 4

                def mm(t, erc=erc, g=g, b_=b_):
                    return t.matmul(bankap(b_),
                                    emat16[:, erc * 128:(erc + 1) * 128],
                                    ar[0:16, O_CFMB + g * 512:O_CFMB + g * 512 + 512],
                                    start=True, stop=True)
                tok_mm = add("tensor", mm, waits=[tok_cfb, bank_tok[b_]])

                def ev(v, erc=erc, g=g, b_=b_):
                    return v.tensor_copy(
                        b16(O_CEXP + erc * 1024 + g * 512, 512), bankap(b_))
                tok_e = add("vector", ev, waits=[tok_mm])
                bank_tok[b_] = tok_e
                tok_cexp.append(tok_e)

        tok_wxa = []
        for erc in range(2):
            for g in range(2):
                b_ = (erc * 2 + g) % 4

                def mm(t, erc=erc, g=g, b_=b_):
                    for fc in range(16):
                        r = t.matmul(bankap(b_),
                                     b16(O_WA + fc * 256 + erc * 128, 128),
                                     b16(O_X2LN + fc * 1024 + g * 512, 512),
                                     start=(fc == 0), stop=(fc == 15))
                    return r
                tok_mm = add("tensor", mm, waits=[tok_ldffn1, bank_tok[b_]])

                def ev(v, erc=erc, g=g, b_=b_):
                    return v.tensor_copy(b16(O_XA + erc * 1024 + g * 512, 512),
                                         bankap(b_))
                tok_e = add("vector", ev, waits=[tok_mm])
                bank_tok[b_] = tok_e

                def wx(v, erc=erc, g=g):
                    off = erc * 1024 + g * 512
                    return v.tensor_tensor(b16(O_WXA + off, 512),
                                           b16(O_XA + off, 512),
                                           b16(O_CEXP + off, 512), OP.mult)
                tok_wxa.append(add("vector", wx,
                                   waits=[tok_cexp[erc * 2 + g]]))

        fence()
        if debug and debug >= 6:
            def d6(sy, D):
                sy.dma_start(dbg["d_lgt"], lgt_tok[:]).then_inc(D(), 16)
                sy.dma_start(dbg["d_coef"], coef_tok[:]).then_inc(D(), 16)
                sy.dma_start(dbg["d_xa"], b16(O_XA, 2048)).then_inc(D(), 16)
                sy.dma_start(dbg["d_wxa"], b16(O_WXA, 2048)).then_inc(D(), 16)
                sy.dma_start(dbg["d_h"], b16(O_HSB, 1024)).then_inc(D(), 16)
            add("sync", d6, n_dma=5)

        if PH < 8 and not PH_OFF[0]:
            def ydummy8(sy, D):
                for r_ in range(8):
                    sy.dma_start(y[r_ * 128:(r_ + 1) * 128, :], xf[0][:]
                                 ).then_inc(D(), 16)
            add("sync", ydummy8, n_dma=8)
            ops["sync"].append(("w", None, [(f"dq{li}", 16 * ((dq_next[0] - 1 - li) // 58 + 1)) for li in range(min(58, dq_next[0]))], None))
            PH_OFF[0] = True
        def ldwf(g_, D):
            g_.dma_start(b16(O_WFFN, 16384), w_in["wfa"]).then_inc(D(), 16)
        tok_ldwf = add("gpsimd", ldwf, n_dma=1)

        bank_tok = [None] * 8
        out_tok = [None] * 8
        tok_wfb = None
        last_oga_mm = None
        n_ydma = 0
        for og in range(4):
            if og == 2:
                def ldwfb(g_, D):
                    g_.dma_start(b16(O_WFFN, 16384), w_in["wfb"]
                                 ).then_inc(D(), 16)
                tok_wfb = add("gpsimd", ldwfb, waits=[last_oga_mm], n_dma=1)
            for tt in range(8):
                b_ = (og * 8 + tt) % 8
                ob_ = (og * 8 + tt) % 8

                def mm(t, og=og, tt=tt, b_=b_):
                    for fc in range(16):
                        t.matmul(bankap(b_),
                                 b16(O_X2LN + fc * 1024 + tt * 128, 128),
                                 b16(O_WFFN + fc * 1024 + (og % 2) * 512, 512),
                                 start=(fc == 0), stop=False)
                    for erc in range(2):
                        t.matmul(bankap(b_),
                                 b16(O_WXA + erc * 1024 + tt * 128, 128),
                                 b16(O_WB + erc * 2048 + og * 512, 512),
                                 start=False, stop=False)
                    return t.matmul(bankap(b_), ones16[:],
                                    brf16[0:1, og * 512:og * 512 + 512],
                                    start=False, stop=True)
                tok_mm = add("tensor", mm,
                             waits=[tok_ldwf if og < 2 else tok_wfb,
                                    bank_tok[b_]] +
                             (tok_wxa if (og == 0 and tt == 0) else []))
                if og == 1 and tt == 7:
                    last_oga_mm = tok_mm

                def ev(v, og=og, tt=tt, b_=b_, ob_=ob_):
                    return v.tensor_tensor(
                        xf[ob_ // 4][:, (ob_ % 4) * 512:(ob_ % 4) * 512 + 512],
                        bankap(b_),
                        f32v(O_X2, 16384)[:, tt * 2048 + og * 512:
                                          tt * 2048 + og * 512 + 512],
                        OP.add)
                tok_e = add("vector", ev, waits=[tok_mm, out_tok[ob_]])
                bank_tok[b_] = tok_e

                def dy(sy, D, og=og, tt=tt, ob_=ob_):
                    sy.dma_start(y[tt * 128:(tt + 1) * 128,
                                   og * 512:og * 512 + 512],
                                 xf[ob_ // 4][:, (ob_ % 4) * 512:
                                              (ob_ % 4) * 512 + 512]
                                 ).then_inc(D(), 16)
                tok_d = add("sync", dy, waits=[tok_e], n_dma=1)
                out_tok[ob_] = tok_d
                n_ydma += 1

        ops["sync"].append(("w", None,
                            [(f"dq{li}", 16 * ((dq_next[0] - 1 - li) // 58 + 1))
                             for li in range(min(58, dq_next[0]))] +
                            [(f"dq{58 + li}", 16) for li in range(dq_next[1])],
                            None))

        for eng in ENGS:
            deco = getattr(blk, eng)
            myops = ops[eng]

            def runner(hh, myops=myops, eng=eng):
                sn = SEM_OF.get(eng)
                for opi, (kind, fn, waits, plan) in enumerate(myops):
                    if any(w is None or not isinstance(w, tuple) for w in waits):
                        raise RuntimeError(f"bad waits {eng} op{opi}: {waits}")
                    for wsn, val in waits:
                        hh.wait_ge(sems[wsn], val)
                    if kind == "cmp":
                        if plan is not None and plan > 0:
                            hh.wait_ge(sems[sn], plan)
                        instr = fn(hh)
                        instr.then_inc(sems[sn], 1)
                    elif kind == "dma":
                        it = iter(plan)

                        def D(hh=hh, it=it):
                            lane, frm = next(it)
                            if frm > 0:
                                hh.wait_ge(sems[lane], frm)
                            return sems[lane]
                        fn(hh, D)
            deco(runner)

    return nc, dbg


def _chunk128(a):
    r, c = a.shape
    assert r % 128 == 0
    return np.ascontiguousarray(
        a.reshape(r // 128, 128, c).transpose(1, 0, 2).reshape(128, -1))


def _prep_shared(i):
    f = np.float32
    out = {}
    out["identf"] = np.eye(128, dtype=f)
    bp = np.zeros((128, 68), f)
    bp[:, BC_KV:BC_KV + 4] = i["L0_attn_bkv_d"].reshape(4, 128).T
    bp[:, BC_QD:BC_QD + 4] = i["L0_attn_bq_d"].reshape(4, 128).T
    bp[0:64, BC_KU:BC_KU + 16] = i["L0_attn_bk_u"].reshape(16, 64).T
    bp[0:64, BC_QU:BC_QU + 16] = i["L0_attn_bq_u"].reshape(16, 64).T
    bp[:, BC_RK:BC_RK + 8] = i["L0_attn_brk"].reshape(8, 128).T
    bp[0:64, BC_RQ:BC_RQ + 16] = i["L0_attn_brq"].reshape(16, 64).T
    bp[0:64, BC_R1] = i["L0_ffn_rb1"]
    bp[0:32, BC_R2] = i["L0_ffn_rb2"]
    bp[0:16, BC_G] = i["L0_ffn_gates"]
    out["bias_pp"] = bp
    out["brv"] = i["L0_attn_bv_u"].reshape(1, E).astype(f)
    out["bro"] = i["L0_attn_bo"].reshape(1, E).astype(f)
    out["brf"] = i["L0_ffn_bias"].reshape(1, E).astype(f)
    em = np.zeros((16, 256), f)
    for e_ in range(16):
        em[e_, e_ * 16:(e_ + 1) * 16] = 1.0
    out["emat"] = em
    rotm = np.zeros((128, 128), f)
    for blk0 in (0, 64):
        for d in range(32):
            rotm[blk0 + d + 32, blk0 + d] = -1.0
            rotm[blk0 + d, blk0 + d + 32] = 1.0
    out["rotm"] = rotm
    out["wkv"] = _chunk128(i["L0_attn_Wkv_d"].T.astype(f))
    out["wqd"] = _chunk128(i["L0_attn_Wq_d"].T.astype(f))
    wrkT = _chunk128(i["L0_attn_Wrk"].T.astype(f))
    wrk = np.zeros((128, 16384), f)
    for oc in range(8):
        for fc in range(16):
            wrk[:, oc * 2048 + fc * 128:oc * 2048 + (fc + 1) * 128] = \
                wrkT[:, fc * 1024 + oc * 128:fc * 1024 + (oc + 1) * 128]
    out["wrk"] = np.ascontiguousarray(wrk)
    out["wku"] = _chunk128(i["L0_attn_Wk_u"].T.astype(f))
    out["wqu"] = _chunk128(i["L0_attn_Wq_u"].T.astype(f))
    out["wvu"] = _chunk128(i["L0_attn_Wv_u"].T.astype(f))
    out["wrq"] = _chunk128(i["L0_attn_Wrq"].T.astype(f))
    woT = _chunk128(i["L0_attn_Wo"].T.astype(f))
    out["woa"] = np.ascontiguousarray(
        woT.reshape(128, 16, 2048)[:, :, 0:1024].reshape(128, -1))
    out["wob"] = np.ascontiguousarray(
        woT.reshape(128, 16, 2048)[:, :, 1024:2048].reshape(128, -1))
    wfT = _chunk128(i["L0_ffn_W"].T.astype(f))
    out["wfa"] = np.ascontiguousarray(
        wfT.reshape(128, 16, 2048)[:, :, 0:1024].reshape(128, -1))
    out["wfb"] = np.ascontiguousarray(
        wfT.reshape(128, 16, 2048)[:, :, 1024:2048].reshape(128, -1))
    out["wa"] = _chunk128(np.ascontiguousarray(
        i["L0_ffn_A"].reshape(256, E).T.astype(f)))
    out["wb"] = _chunk128(np.ascontiguousarray(
        i["L0_ffn_B"].transpose(0, 2, 1).reshape(256, E).astype(f)))
    out["wr1"] = _chunk128(i["L0_ffn_rw1"].T.astype(f))
    out["wr2"] = np.ascontiguousarray(i["L0_ffn_rw2"].T.astype(f))
    return out


def _rope_tables(pos):
    inv = (1.0 / 10000.0 ** (np.arange(0, R, 2, dtype=np.float64) / R)
           ).astype(np.float32)
    t = pos.astype(np.float32)
    fr = np.outer(t, inv)
    emb = np.concatenate([fr, fr], axis=-1)
    cos = np.cos(emb).astype(np.float32)
    sin = np.sin(emb).astype(np.float32)
    cc = np.zeros((128, len(pos)), np.float32)
    sg = np.zeros((128, len(pos)), np.float32)
    for p in range(128):
        d = p % 64
        cc[p] = cos[:, d]
        sg[p] = sin[:, d]
    return cc, sg


def kernel(**inputs):
    shared = _prep_shared(inputs)
    x = np.asarray(inputs["L0_attn_x"], dtype=np.float32)
    nc, _ = build(debug=False)
    in_maps = []
    for c in range(8):
        b_, th = c // 2, c % 2
        if th == 0:
            xcore = np.ascontiguousarray(x[b_])
            pos = np.arange(S)
        else:
            xcore = np.ascontiguousarray(
                np.concatenate([x[b_, TOK:], x[b_, :TOK]], axis=0))
            pos = np.concatenate([np.arange(TOK, S), np.arange(TOK)])
        cc, sg = _rope_tables(pos)
        m = dict(shared)
        m["xc"] = xcore
        m["cc"] = cc
        m["sgn"] = sg
        in_maps.append(m)
    res = run_bass_kernel_spmd(nc, in_maps, core_ids=list(range(8)))
    out = np.empty((B, S, E), dtype=np.float32)
    for c in range(8):
        b_, th = c // 2, c % 2
        out[b_, th * TOK:(th + 1) * TOK, :] = res.results[c]["y"]
    return out


# version 04
# speedup vs baseline: 1.0764x; 1.0129x over previous
import math
import os as _os
import contextlib
import numpy as np
import concourse.bass as bass
import concourse.mybir as mybir
from concourse.bass_utils import run_bass_kernel_spmd

F32 = mybir.dt.float32
BF16 = mybir.dt.bfloat16
OP = mybir.AluOpType
AF = mybir.ActivationFunctionType

B, S, E = 4, 2048, 2048
H, LAT, D, R = 16, 512, 128, 64
TOK = 1024
SHIFT = 12.0
SCALE = 1.0 / math.sqrt(D)
EPS = 1e-5
AC = 88064

O_XLNFM, O_WKV, O_WQD, O_WRKH = 0, 32768, 40960, 49152
O_XLT, O_KRT = 57344, 61440
O_CC, O_SGN = 63488, 65536
O_KW, O_VW, O_QW = 0, 16384, 32896
O_PT0, O_PT1, O_UPS = 41088, 45184, 49280
O_KV, O_QD, O_ATTN = 57472, 65664, 69760
O_O2SB, O_QRT = 86144, 86400
O_WOA, O_X2, O_WOB = 0, 16384, 49152
O_X2LN, O_WFFN = 0, 49152
O_WA, O_WB, O_WR1, O_HSB = 65536, 69632, 73728, 74752
O_CEXP, O_XA, O_WXA = 75776, 77824, 79872
O_LGT, O_CFM = 81920, 83968
O_CFMB, O_TOKB = 73728, 86016

BC_KV, BC_QD, BC_KU, BC_QU, BC_RK, BC_RQ = 0, 4, 8, 24, 40, 48
BC_R1, BC_R2, BC_G = 64, 65, 66

ENGS = ("sync", "gpsimd", "tensor", "vector", "scalar")
SEM_OF = dict(sync="sd", gpsimd="sg", tensor="t", vector="v", scalar="s")


def build(debug=False):
    PH = int(_os.environ.get("BASS_PH", "99"))
    nc = bass.Bass("TRN2", target_bir_lowering=False)

    xc = nc.dram_tensor("xc", [S, E], F32, kind="ExternalInput").ap()
    cc_in = nc.dram_tensor("cc", [128, S], F32, kind="ExternalInput").ap()
    sgn_in = nc.dram_tensor("sgn", [128, S], F32, kind="ExternalInput").ap()
    identf_in = nc.dram_tensor("identf", [128, 128], F32, kind="ExternalInput").ap()
    bias_pp_in = nc.dram_tensor("bias_pp", [128, 68], F32, kind="ExternalInput").ap()
    brv_in = nc.dram_tensor("brv", [1, E], F32, kind="ExternalInput").ap()
    bro_in = nc.dram_tensor("bro", [1, E], F32, kind="ExternalInput").ap()
    brf_in = nc.dram_tensor("brf", [1, E], F32, kind="ExternalInput").ap()
    emat_in = nc.dram_tensor("emat", [16, 256], F32, kind="ExternalInput").ap()
    rotm_in = nc.dram_tensor("rotm", [128, 128], F32, kind="ExternalInput").ap()
    w_in = {}
    w_th = {}
    for name, shp in [("wkv", [128, 8192]), ("wqd", [128, 8192]),
                      ("wrk", [128, 16384]), ("wku", [128, 4096]),
                      ("wqu", [128, 4096]), ("wvu", [128, 8192]),
                      ("wrq", [128, 4096]), ("woa", [128, 16384]),
                      ("wob", [128, 16384]), ("wfa", [128, 16384]),
                      ("wfb", [128, 16384]), ("wa", [128, 4096]),
                      ("wb", [128, 4096]), ("wr1", [128, 1024]),
                      ("wr2", [64, 32])]:
        w_th[name] = nc.dram_tensor(name, shp, F32, kind="ExternalInput")
        w_in[name] = w_th[name].ap()
    y = nc.dram_tensor("y", [TOK, E], F32, kind="ExternalOutput").ap()
    kv_hbm = nc.dram_tensor("kv_hbm", [128, 8192], BF16).ap()
    qd_hbm = nc.dram_tensor("qd_hbm", [128, 4096], BF16).ap()
    kr_hbm = nc.dram_tensor("kr_hbm", [128, 16384], BF16).ap()

    dbg = {}
    _dbg_shapes = {"d_xlnfm": ([128, 32768], BF16), "d_kv": ([128, 8192], BF16),
                   "d_qd": ([128, 4096], BF16), "d_kr": ([128, 16384], BF16),
                   "d_kw": ([128, 16384], BF16), "d_vw": ([128, 16512], BF16),
                   "d_qw": ([128, 8192], BF16), "d_attn": ([128, 16384], BF16),
                   "d_x2": ([128, 16384], F32), "d_x2ln": ([128, 16384], BF16),
                   "d_lgt": ([128, 128], F32), "d_coef": ([128, 128], F32),
                   "d_xa": ([128, 2048], BF16), "d_wxa": ([128, 2048], BF16),
                   "d_h": ([128, 1024], BF16), "d_xf0": ([128, 2048], F32),
                   "d_xlt": ([128, 4096], BF16), "d_lnx": ([128, 10], F32)}

    class _Dbg(dict):
        def __getitem__(self, k):
            if k not in self:
                shp, dt = _dbg_shapes[k]
                self[k] = nc.dram_tensor(k, shp, dt, kind="ExternalOutput").ap()
            return dict.__getitem__(self, k)
    dbg = _Dbg()
    if debug:
        lvl_names = {1: ["d_xlnfm", "d_xf0", "d_xlt"],
                     2: ["d_kv", "d_qd", "d_kr"],
                     3: ["d_kw", "d_vw", "d_qw", "d_attn"],
                     4: ["d_x2"], 5: ["d_x2ln"],
                     6: ["d_lgt", "d_coef", "d_xa", "d_wxa", "d_h"]}
        for lv in range(1, int(debug) + 1):
            for nm in lvl_names.get(lv, []):
                dbg[nm]

    ctx = contextlib.ExitStack()
    en = ctx.enter_context
    with ctx:
        ar = en(nc.sbuf_tensor("arena", [128, AC], BF16))
        ident16 = en(nc.sbuf_tensor("ident16", [128, 128], BF16))
        identf = en(nc.sbuf_tensor("identf_s", [128, 128], F32))
        ccq = en(nc.sbuf_tensor("ccq", [128, TOK], BF16))
        sgnq = en(nc.sbuf_tensor("sgnq", [128, TOK], BF16))
        brv16 = en(nc.sbuf_tensor("brv16", [1, E], BF16))
        bro16 = en(nc.sbuf_tensor("bro16", [1, E], BF16))
        brf16 = en(nc.sbuf_tensor("brf16", [1, E], BF16))
        bias_pp = en(nc.sbuf_tensor("bias_pp_s", [128, 68], F32))
        emat16 = en(nc.sbuf_tensor("emat16", [16, 256], BF16))
        ones16 = en(nc.sbuf_tensor("ones16", [1, 128], BF16))
        wr2_sb = en(nc.sbuf_tensor("wr2_sb", [64, 32], BF16))
        rot16 = en(nc.sbuf_tensor("rot16", [128, 128], BF16))
        consts = en(nc.sbuf_tensor("consts", [128, 8], F32))
        C_EPS = consts[:, 0:1]
        C_NSH = consts[:, 1:2]
        C_SCL = consts[:, 2:3]
        C_NEG2 = consts[:, 3:4]
        C_ONE = consts[:, 4:5]
        C_NEG1 = consts[:, 5:6]
        xf = [en(nc.sbuf_tensor(f"xf{i}", [128, E], F32)) for i in range(2)]
        lnst = en(nc.sbuf_tensor("lnst", [128, 48], F32))
        lnag = en(nc.sbuf_tensor("lnag", [128, 4], F32))
        lnnm = en(nc.sbuf_tensor("lnnm", [128, 2], F32))
        lnri = en(nc.sbuf_tensor("lnri", [128, 2], F32))
        lnr = en(nc.sbuf_tensor("lnr", [128, 2], F32))
        lnr2 = en(nc.sbuf_tensor("lnr2", [128, 2], F32))
        zin = en(nc.sbuf_tensor("zin", [128, 2], F32))
        lgt_tok = en(nc.sbuf_tensor("lgt_tok", [128, 128], F32))
        m8 = en(nc.sbuf_tensor("m8", [128, 8], F32))
        tks = en(nc.sbuf_tensor("tks", [128, 4], F32))
        msk = en(nc.sbuf_tensor("msk", [128, 64], F32))
        junk = en(nc.sbuf_tensor("junk", [128, 8], F32))
        _jc = [0]
        coef_tok = en(nc.sbuf_tensor("coef_tok", [128, 128], F32))
        pA = en(nc.psum_tensor("pA", [128, 2048], F32))
        pB = en(nc.psum_tensor("pB", [128, 2048], F32))
        sems = {k: en(nc.semaphore(k)) for k in ("t", "v", "s")}
        NDQ = 90
        for li in range(NDQ):
            sems[f"dq{li}"] = en(nc.semaphore(f"dq{li}"))
        blk = en(nc.Block("all"))

        ops = {e_: [] for e_ in ENGS}
        cnt = {s_: 0 for s_ in ("t", "v", "s")}
        dq_next = [0, 0]
        lastt = {e_: None for e_ in ENGS}

        PH_OFF = [False]

        def _flat(waits):
            out = []
            for w in waits:
                if w is None:
                    continue
                if isinstance(w, list):
                    out.extend(w)
                else:
                    out.append(w)
            return out

        def jop(v):
            _jc[0] = (_jc[0] + 1) % 8
            return v.memset(junk[:, _jc[0]:_jc[0] + 1], 0.0)

        def add(eng, fn, waits=(), n_dma=None):
            if PH_OFF[0]:
                return None
            ws = _flat(waits)
            if n_dma is not None:
                plan = []
                tok = []
                for _ in range(n_dma):
                    if eng == "gpsimd":
                        li = 58 + dq_next[1]
                        dq_next[1] += 1
                        assert li < 90, "out of gpsimd dma sems"
                        rnd = 0
                    else:
                        li = dq_next[0] % 58
                        rnd = dq_next[0] // 58
                        dq_next[0] += 1
                    lane = f"dq{li}"
                    plan.append((lane, 0))
                    tok.append((lane, 16 * (rnd + 1)))
                ops[eng].append(("dma", fn, ws, plan))
            else:
                sn = SEM_OF[eng]
                selfw = cnt[sn] if eng in ("vector", "scalar") else None
                cnt[sn] += 1
                tok = (sn, cnt[sn])
                ops[eng].append(("cmp", fn, ws, selfw))
            lastt[eng] = tok
            return tok

        def fence():
            toks = _flat([lastt[e_] for e_ in ("tensor", "vector", "scalar")])
            for e_ in ENGS:
                ops[e_].append(("w", None, list(toks), None))

        def b16(off, n):
            assert off + n <= AC
            return ar[:, off:off + n]

        def f32v(off, nf):
            assert off % 2 == 0 and off + 2 * nf <= AC
            return ar[:, off:off + 2 * nf].bitcast(F32)

        def bankap(b_, c0=0, w=512):
            pt = pA if b_ < 4 else pB
            base = (b_ % 4) * 512
            return pt[:, base + c0:base + c0 + w]

        def ph0_sync(sy, D):
            for dst, src in [(identf[:], identf_in), (bias_pp[:], bias_pp_in)]:
                sy.dma_start(dst, src).then_inc(D(), 16)
        t_ph0s = add("sync", ph0_sync, n_dma=2)

        def ph0_g(g, D):
            for dst, src in [
                    (ident16[:], identf_in), (b16(O_CC, S), cc_in),
                    (b16(O_SGN, S), sgn_in),
                    (ccq[:], cc_in[:, 0:TOK]), (sgnq[:], sgn_in[:, 0:TOK]),
                    (brv16[:], brv_in), (bro16[:], bro_in), (brf16[:], brf_in),
                    (emat16[:], emat_in), (wr2_sb[:], w_in["wr2"]),
                    (rot16[:], rotm_in),
                    (b16(O_WKV, 8192), w_in["wkv"]),
                    (b16(O_WQD, 8192), w_in["wqd"]),
                    (b16(O_WRKH, 8192), w_in["wrk"][:, 0:8192])]:
                g.dma_start(dst, src).then_inc(D(), 16)
        t_ph0g = add("gpsimd", ph0_g, n_dma=14)

        def ph0_v2a(v):
            return v.tensor_copy(bias_pp[0:32, 67:68],
                                 bias_pp[0:32, BC_R2:BC_R2 + 1])

        def ph0_v2(v):
            return v.tensor_tensor(bias_pp[0:16, 67:68], bias_pp[0:16, 67:68],
                                   bias_pp[0:16, BC_G:BC_G + 1], OP.add)

        def ph0_v(v):
            v.memset(ones16[:], 1.0)
            v.memset(consts[:, 0:1], EPS)
            v.memset(consts[:, 1:2], -SHIFT)
            v.memset(consts[:, 2:3], SCALE)
            v.memset(consts[:, 3:4], -2.0)
            v.memset(consts[:, 4:5], 1.0)
            return v.memset(consts[:, 5:6], -1.0)
        t_consts = add("vector", ph0_v)
        add("vector", ph0_v2a, waits=[("dq0", 16), ("dq1", 16)])
        t_cb = add("vector", ph0_v2)

        def emit_ln(src_tile_fn, n_tiles, xin_toks, dst_fm_off, dst_fm_stride,
                    tokbuf_off, nbuf_tok, first_tr_waits=()):
            tok_agg = [None] * n_tiles
            tok_st = [None] * n_tiles
            tok_pre = [None] * n_tiles
            tok_sq = [None] * n_tiles
            tok_nrm = [None] * n_tiles
            tok_tr = [None] * n_tiles
            tok_ev = [None] * n_tiles
            for i in range(n_tiles + 2):
                if i >= 2:
                    j = i - 2
                    io = j % 2

                    def preA(v, io=io):
                        v.tensor_scalar(lnri[:, io:io + 1],
                                        lnag[:, 2 * io + 1:2 * io + 2],
                                        C_EPS, None, OP.add)
                        return v.tensor_scalar(lnnm[:, io:io + 1],
                                               lnag[:, 2 * io:2 * io + 1],
                                               C_NEG1, None, OP.mult)
                    add("vector", preA,
                        waits=[tok_agg[j], tok_sq[j - 2] if j >= 2 else None])

                    def preB(v, io=io):
                        return v.reciprocal(lnr2[:, io:io + 1], lnri[:, io:io + 1])
                    tok_pre[j] = add("vector", preB)

                    def sq(sc, io=io):
                        return sc.activation(lnr[:, io:io + 1], lnr2[:, io:io + 1],
                                             AF.Sqrt)
                    tok_sq[j] = add("scalar", sq,
                                    waits=[tok_pre[j],
                                           tok_nrm[j - 2] if j >= 2 else None])

                    tb = j % nbuf_tok

                    def nrm(v, j=j, io=io, tb=tb):
                        return v.tensor_scalar(
                            b16(tokbuf_off + tb * 2048, 2048), src_tile_fn(j),
                            lnnm[:, io:io + 1], lnr[:, io:io + 1],
                            OP.add, OP.mult)
                    tok_nrm[j] = add(
                        "vector", nrm,
                        waits=[tok_sq[j],
                               tok_ev[j - nbuf_tok] if j >= nbuf_tok else None])

                    def tr(t, tb=tb):
                        for fc in range(16):
                            b_ = fc % 8
                            reg = fc // 8
                            r = t.matmul(bankap(b_, reg * 128, 128),
                                         b16(tokbuf_off + tb * 2048 + fc * 128, 128),
                                         ident16[:], start=True, stop=True)
                        return r
                    tok_tr[j] = add("tensor", tr,
                                    waits=[tok_nrm[j],
                                           tok_ev[j - 1] if j >= 1 else None] +
                                    (list(first_tr_waits) if j == 0 else []))

                    def ev(sc, j=j):
                        for pi, pt in enumerate([pA, pB]):
                            dst = bass.AP(
                                ar, dst_fm_off + j * 128 +
                                pi * 4 * dst_fm_stride,
                                [[AC, 128], [dst_fm_stride, 4],
                                 [8 * dst_fm_stride, 2], [1, 128]])
                            src = bass.AP(pt, 0,
                                          [[2048, 128], [512, 4], [128, 2],
                                           [1, 128]])
                            r = sc.activation(dst, src, AF.Copy)
                        return r
                    tok_ev[j] = add("scalar", ev, waits=[tok_tr[j]])
                if 1 <= i <= n_tiles:
                    jj = i - 1
                    if jj <= 1:
                        def spacer(v):
                            jop(v)
                            jop(v)
                            return jop(v)
                        add("vector", spacer)

                    def agg(v, jj=jj):
                        return v.bn_aggr(lnag[:, 2 * (jj % 2):2 * (jj % 2) + 2],
                                         lnst[:, (jj % 2) * 24:(jj % 2) * 24 + 24])
                    tok_agg[jj] = add("vector", agg,
                                      waits=[tok_st[jj],
                                             tok_pre[jj - 2] if jj >= 2 else None])
                if i < n_tiles:
                    def stats(v, i=i):
                        src = src_tile_fn(i)
                        base = (i % 2) * 24
                        for j2 in range(4):
                            r = v.bn_stats(lnst[:, base + 6 * j2:base + 6 * j2 + 6],
                                           src[:, 512 * j2:512 * j2 + 512])
                        return r
                    tok_st[i] = add("vector", stats,
                                    waits=[xin_toks[i] if xin_toks else None,
                                           t_consts if i == 0 else None,
                                           tok_agg[i - 2] if i >= 2 else None])
            return tok_ev, tok_nrm

        NT1 = 16
        xf3 = [xf[0][:], xf[1][:], f32v(69632, 2048)]
        tok_xin = [None] * NT1
        for i in range(NT1):
            def xin(sy, D, i=i):
                sy.dma_start(xf3[i % 3], xc[i * 128:(i + 1) * 128, :]
                             ).then_inc(D(), 16)
            tok_xin[i] = add("sync", xin, n_dma=1)
        ev1_toks, nrm1_toks = emit_ln(lambda i: xf3[i % 3], NT1, tok_xin,
                                      O_XLNFM, 2048, O_XLT, 2,
                                      first_tr_waits=[t_ph0g, t_ph0s])
        ph1_sync_ops = [o for o in ops["sync"] if o[0] == "dma"][-NT1:]
        for i in range(3, NT1):
            ph1_sync_ops[i][2].append(nrm1_toks[i - 3])

        fence()

        if debug and debug >= 1:
            _dm = _os.environ.get("DBG1", "abc")

            def d1(sy, D):
                if "a" in _dm:
                    sy.dma_start(dbg["d_xlnfm"], b16(O_XLNFM, 32768)
                                 ).then_inc(D(), 16)
                if "b" in _dm:
                    sy.dma_start(dbg["d_xf0"], xf[0][:]).then_inc(D(), 16)
                if "c" in _dm:
                    sy.dma_start(dbg["d_xlt"], b16(O_XLT, 4096)
                                 ).then_inc(D(), 16)
            add("sync", d1, n_dma=len(_dm))


        if PH < 2:
            def ydummy(sy, D):
                for r_ in range(8):
                    sy.dma_start(y[r_ * 128:(r_ + 1) * 128, :], xf[0][:]
                                 ).then_inc(D(), 16)
            add("sync", ydummy, n_dma=8)
            ops["sync"].append(("w", None, [(f"dq{li}", 16 * ((dq_next[0] - 1 - li) // 58 + 1)) for li in range(min(58, dq_next[0]))], None))
            PH_OFF[0] = True
        bank_tok = [None] * 8
        tmp_tok = [None] * 4
        last_dma = {"kv": [], "qd": [], "kr": []}

        def proj_job(wcol_fn, rhs_off_fn, bias_ap, out_hbm, out_col, jid,
                     extra_waits=(), rope=False, rope_g=None, rope_oc=None):
            b_ = jid % 6
            tb = jid % 2
            rb = 2 + jid % 2

            def mm(t, b_=b_):
                for fc in range(16):
                    r = t.matmul(bankap(b_), b16(wcol_fn(fc), 128),
                                 b16(rhs_off_fn(fc), 512),
                                 start=(fc == 0), stop=(fc == 15))
                return r
            tok_mm = add("tensor", mm, waits=[bank_tok[b_]] + list(extra_waits))

            if not rope:
                def ev(v, b_=b_, tb=tb, bias_ap=bias_ap):
                    return v.tensor_scalar(b16(O_KRT + tb * 512, 512),
                                           bankap(b_), bias_ap, None, OP.add)
                tok_e = add("vector", ev,
                            waits=[tok_mm, tmp_tok[tb]] +
                            ([t_ph0s, t_ph0g] if jid == 0 else []))
                src_off = O_KRT + tb * 512
                bufslot = tb
                bank_tok[b_] = tok_e
            else:
                rbank = 6 + jid % 2

                def eva(v, b_=b_, tb=tb, bias_ap=bias_ap):
                    kt = b16(O_KRT + tb * 512, 512)
                    return v.tensor_scalar(kt, bankap(b_), bias_ap, None, OP.add)
                tok_a = add("vector", eva, waits=[tok_mm, tmp_tok[tb],
                                                  tmp_tok[rb]])

                def rmm(t, tb=tb, rbank=rbank):
                    return t.matmul(bankap(rbank), rot16[:],
                                    b16(O_KRT + tb * 512, 512),
                                    start=True, stop=True)
                tok_r = add("tensor", rmm, waits=[tok_a, bank_tok[rbank]])

                def evb1a(v, tb=tb, rb=rb, g=rope_g):
                    return v.tensor_tensor(b16(O_KRT + rb * 512, 512),
                                           b16(O_KRT + tb * 512, 512),
                                           b16(O_CC + g * 512, 512), OP.mult)
                add("vector", evb1a, waits=[tok_r])

                def evb1b(v, tb=tb, g=rope_g, rbank=rbank):
                    return v.tensor_tensor(b16(O_KRT + tb * 512, 512),
                                           bankap(rbank),
                                           b16(O_SGN + g * 512, 512), OP.mult)
                add("vector", evb1b)

                def evb2(v, tb=tb, rb=rb):
                    return v.tensor_tensor(b16(O_KRT + rb * 512, 512),
                                           b16(O_KRT + rb * 512, 512),
                                           b16(O_KRT + tb * 512, 512), OP.add)
                tok_e = add("vector", evb2)
                tmp_tok[tb] = tok_e
                src_off = O_KRT + rb * 512
                bufslot = rb
                bank_tok[b_] = tok_a
                bank_tok[rbank] = tok_e

            def dout(sy, D, src_off=src_off, out_col=out_col):
                sy.dma_start(out_hbm[:, out_col:out_col + 512],
                             b16(src_off, 512)).then_inc(D(), 16)
            tok_d = add("sync", dout, waits=[tok_e], n_dma=1)
            tmp_tok[bufslot] = tok_d
            return tok_mm, tok_d

        jid = 0
        for lc in range(4):
            for g in range(4):
                _, d_ = proj_job(lambda fc, lc=lc: O_WKV + fc * 512 + lc * 128,
                                 lambda fc, g=g: O_XLNFM + fc * 2048 + g * 512,
                                 bias_pp[:, BC_KV + lc:BC_KV + lc + 1],
                                 kv_hbm, lc * 2048 + g * 512, jid,
                                 extra_waits=[t_ph0g] if jid == 0 else ())
                last_dma["kv"].extend(d_)
                jid += 1
        for lc in range(4):
            for g in range(2):
                _, d_ = proj_job(lambda fc, lc=lc: O_WQD + fc * 512 + lc * 128,
                                 lambda fc, g=g: O_XLNFM + fc * 2048 + g * 512,
                                 bias_pp[:, BC_QD + lc:BC_QD + lc + 1],
                                 qd_hbm, lc * 1024 + g * 512, jid,
                                 extra_waits=[t_ph0g] if (lc == 0 and g == 0) else ())
                last_dma["qd"].extend(d_)
                jid += 1
        tok_wrk2 = None
        last_half0_mm = None
        for oc in range(8):
            if oc == 4:
                def wrk2(g_, D):
                    g_.dma_start(b16(O_WRKH, 8192), w_in["wrk"][:, 8192:16384]
                                 ).then_inc(D(), 16)
                tok_wrk2 = add("gpsimd", wrk2, waits=[last_half0_mm], n_dma=1)
            for g in range(4):
                m_, d_ = proj_job(
                    lambda fc, oc=oc: O_WRKH + (oc % 4) * 2048 + fc * 128,
                    lambda fc, g=g: O_XLNFM + fc * 2048 + g * 512,
                    bias_pp[:, BC_RK + oc:BC_RK + oc + 1],
                    kr_hbm, oc * 2048 + g * 512, jid,
                    extra_waits=([tok_wrk2] if oc >= 4 else []) +
                    ([t_ph0g] if (oc == 0 and g == 0) else []),
                    rope=True, rope_g=g, rope_oc=oc)
                last_dma["kr"].extend(d_)
                jid += 1
                if oc == 3 and g == 3:
                    last_half0_mm = m_

        fence()

        if debug and debug >= 2:
            def d2(sy, D):
                sy.dma_start(dbg["d_kv"], kv_hbm).then_inc(D(), 16)
                sy.dma_start(dbg["d_qd"], qd_hbm).then_inc(D(), 16)
                sy.dma_start(dbg["d_kr"], kr_hbm).then_inc(D(), 16)
            add("sync", d2, waits=[last_dma["kv"], last_dma["qd"],
                                   last_dma["kr"]], n_dma=3)

        if PH < 3 and not PH_OFF[0]:
            def ydummy3(sy, D):
                for r_ in range(8):
                    sy.dma_start(y[r_ * 128:(r_ + 1) * 128, :], xf[0][:]
                                 ).then_inc(D(), 16)
            add("sync", ydummy3, n_dma=8)
            ops["sync"].append(("w", None, [(f"dq{li}", 16 * ((dq_next[0] - 1 - li) // 58 + 1)) for li in range(min(58, dq_next[0]))], None))
            PH_OFF[0] = True
        def ldkv(sy, D):
            sy.dma_start(b16(O_KV, 8192), kv_hbm).then_inc(D(), 16)
            sy.dma_start(b16(O_QD, 4096), qd_hbm).then_inc(D(), 16)
        tok_ldkv = add("sync", ldkv,
                       waits=last_dma["kv"] + last_dma["qd"] + last_dma["kr"],
                       n_dma=2)

        bank_tok = [None] * 8
        prev = {"exB1": None, "exB2": None, "ev1": None, "ev2": None,
                "pv": [None, None]}
        pend = None
        tok_attn_done = None
        u_glob = 0

        for w in range(2):
            def ldku(g_, D):
                g_.dma_start(b16(O_UPS, 4096), w_in["wku"]).then_inc(D(), 16)
                g_.dma_start(b16(O_UPS + 4096, 4096), w_in["wqu"]
                             ).then_inc(D(), 16)
            tok_ldku = add("gpsimd", ldku,
                           waits=[tok_attn_done if w else t_ph0g], n_dma=2)

            wave_toks = []

            jb = 0
            last_k1_ev = None
            for hw in range(8):
                for g in range(4):
                    b_ = jb % 6
                    h_ = 8 * w + hw

                    def mm(t, hw=hw, g=g, b_=b_, h_=h_):
                        for lc in range(4):
                            r = t.matmul(bankap(b_)[0:64, :],
                                         b16(O_UPS + lc * 1024 + h_ * 64, 64),
                                         b16(O_KV + lc * 2048 + g * 512, 512),
                                         start=(lc == 0), stop=(lc == 3))
                        return r
                    tok_mm = add("tensor", mm,
                                 waits=[tok_ldku, bank_tok[b_],
                                        tok_ldkv if jb == 0 else None,
                                        tok_attn_done if (w and jb == 0) else None])

                    def ev(v, hw=hw, g=g, b_=b_, h_=h_):
                        return v.tensor_scalar(
                            ar[0:64, O_KW + hw * 2048 + g * 512:
                               O_KW + hw * 2048 + g * 512 + 512],
                            bankap(b_)[0:64, :],
                            bias_pp[0:64, BC_KU + h_:BC_KU + h_ + 1],
                            None, OP.add)
                    tok_e = add("vector", ev, waits=[tok_mm])
                    bank_tok[b_] = tok_e
                    wave_toks.append(tok_e)
                    last_k1_ev = tok_e
                    jb += 1

            for hw in range(8):
                for g in range(2):
                    b_ = jb % 6
                    h_ = 8 * w + hw

                    def mm(t, g=g, b_=b_, h_=h_):
                        for lc in range(4):
                            r = t.matmul(bankap(b_)[0:64, :],
                                         b16(O_UPS + 4096 + lc * 1024 + h_ * 64, 64),
                                         b16(O_QD + lc * 1024 + g * 512, 512),
                                         start=(lc == 0), stop=(lc == 3))
                        return r
                    tok_mm = add("tensor", mm, waits=[bank_tok[b_]])

                    def ev(v, hw=hw, g=g, b_=b_, h_=h_):
                        return v.tensor_scalar(
                            ar[0:64, O_QW + hw * 1024 + g * 512:
                               O_QW + hw * 1024 + g * 512 + 512],
                            bankap(b_)[0:64, :],
                            bias_pp[0:64, BC_QU + h_:BC_QU + h_ + 1],
                            None, OP.add)
                    tok_e = add("vector", ev, waits=[tok_mm])
                    bank_tok[b_] = tok_e
                    wave_toks.append(tok_e)
                    jb += 1

            def ldrq(g_, D):
                g_.dma_start(b16(O_UPS, 4096), w_in["wrq"]).then_inc(D(), 16)
            tok_ldrq = add("gpsimd", ldrq, waits=[last_k1_ev], n_dma=1)

            last_qr_ev = None
            qr_dma_toks = [None, None]
            qidx = 0
            for hw in range(8):
                for g in range(2):
                    b_ = jb % 6
                    h_ = 8 * w + hw

                    def mm(t, g=g, b_=b_, h_=h_):
                        for lc in range(4):
                            r = t.matmul(bankap(b_)[0:64, :],
                                         b16(O_UPS + lc * 1024 + h_ * 64, 64),
                                         b16(O_QD + lc * 1024 + g * 512, 512),
                                         start=(lc == 0), stop=(lc == 3))
                        return r
                    tok_mm = add("tensor", mm, waits=[tok_ldrq, bank_tok[b_]])
                    rbank = 6 + jb % 2

                    def eva(v, b_=b_, h_=h_):
                        kt = b16(O_QRT, 512)
                        return v.tensor_scalar(
                            kt[0:64, :], bankap(b_)[0:64, :],
                            bias_pp[0:64, BC_RQ + h_:BC_RQ + h_ + 1],
                            None, OP.add)
                    tok_a = add("vector", eva, waits=[tok_mm, last_qr_ev])

                    def rmm(t, rbank=rbank):
                        return t.matmul(bankap(rbank)[0:64, :],
                                        rot16[0:64, 0:64],
                                        b16(O_QRT, 512)[0:64, :],
                                        start=True, stop=True)
                    tok_r = add("tensor", rmm, waits=[tok_a, bank_tok[rbank]])

                    kslot = qidx % 2

                    def evb1a(v, g=g, kslot=kslot):
                        ko = b16(O_QRT + 512 + kslot * 512, 512)
                        return v.tensor_tensor(
                            ko[0:64, :], b16(O_QRT, 512)[0:64, :],
                            ccq[0:64, g * 512:g * 512 + 512], OP.mult)
                    add("vector", evb1a, waits=[tok_r, qr_dma_toks[kslot]])

                    def evb1b(v, g=g, rbank=rbank):
                        return v.tensor_tensor(
                            b16(O_QRT, 512)[0:64, :], bankap(rbank)[0:64, :],
                            sgnq[0:64, g * 512:g * 512 + 512], OP.mult)
                    add("vector", evb1b)

                    def evb2(v, kslot=kslot):
                        ko = b16(O_QRT + 512 + kslot * 512, 512)
                        return v.tensor_tensor(ko[0:64, :], ko[0:64, :],
                                               b16(O_QRT, 512)[0:64, :], OP.add)
                    tok_e = add("vector", evb2)
                    bank_tok[b_] = tok_a
                    bank_tok[rbank] = tok_e

                    def qmv(sy, D, hw=hw, g=g, kslot=kslot):
                        sy.dma_start(
                            ar[64:128, O_QW + hw * 1024 + g * 512:
                               O_QW + hw * 1024 + g * 512 + 512],
                            ar[0:64, O_QRT + 512 + kslot * 512:
                               O_QRT + 1024 + kslot * 512]
                        ).then_inc(D(), 16)
                    qr_dma_toks[kslot] = add("sync", qmv, waits=[tok_e], n_dma=1)
                    wave_toks.append(qr_dma_toks[kslot])
                    last_qr_ev = tok_e
                    qidx += 1
                    jb += 1

            def ldkr(sy, D, w=w):
                for hw in range(8):
                    h_ = 8 * w + hw
                    oc, h2 = h_ // 2, h_ % 2
                    src = kr_hbm[h2 * 64:h2 * 64 + 64, oc * 2048:(oc + 1) * 2048]
                    dst = ar[64:128, O_KW + hw * 2048:O_KW + (hw + 1) * 2048]
                    sy.dma_start(dst, src).then_inc(D(), 16)
            tok_ldkr = add("sync", ldkr,
                           waits=last_dma["kr"] +
                           [tok_attn_done if w else None], n_dma=8)
            wave_toks.append(tok_ldkr)

            def ldvu_g(g_, D, w=w):
                for lp in range(2):
                    dst = bass.AP(ar, O_UPS + lp * 4096 + w * 1024,
                                  [[AC, 128], [2048, 2], [1, 1024]])
                    src = bass.AP(w_th["wvu"], lp * 4096 + w * 1024,
                                  [[8192, 128], [2048, 2], [1, 1024]])
                    g_.dma_start(dst, src).then_inc(D(), 16)
            tok_ldvu_g = add("gpsimd", ldvu_g, waits=[last_qr_ev], n_dma=2)
            tok_ldvu_s = None

            def vones(v):
                return v.memset(bass.AP(ar, O_VW + 128,
                                        [[AC, 128], [129, 128], [1, 1]]), 1.0)
            tok_vone = add("vector", vones,
                           waits=[tok_attn_done if w else None])
            wave_toks.append(tok_vone)

            for tc in range(16):
                for dg in range(2):
                    b_ = jb % 6

                    def mm(t, tc=tc, dg=dg, b_=b_, w=w):
                        for lc in range(4):
                            t.matmul(bankap(b_), b16(O_KV + lc * 2048 + tc * 128, 128),
                                     b16(O_UPS + lc * 2048 + w * 1024 + dg * 512, 512),
                                     start=(lc == 0), stop=False)
                        return t.matmul(bankap(b_), ones16[:],
                                        brv16[0:1, w * 1024 + dg * 512:
                                              w * 1024 + dg * 512 + 512],
                                        start=False, stop=True)
                    tok_mm = add("tensor", mm,
                                 waits=[tok_ldvu_g, tok_ldvu_s, bank_tok[b_]])

                    def ev(v, tc=tc, dg=dg, b_=b_):
                        dst = bass.AP(ar, O_VW + (tc * 8 + dg * 4) * 129,
                                      [[AC, 128], [129, 4], [1, 128]])
                        return v.tensor_copy(dst, bankap(b_))
                    tok_e = add("vector", ev, waits=[tok_mm, tok_vone])
                    bank_tok[b_] = tok_e
                    wave_toks.append(tok_e)
                    jb += 1

            for hw in range(8):
                for g in range(4):
                    u = u_glob
                    p = u % 2
                    ORD = [0, 1, 2, 3, 4, 5, 6, 7] if p == 0 else \
                          [4, 5, 6, 7, 0, 1, 2, 3]
                    pt_off = O_PT0 if p == 0 else O_PT1
                    pHA = pA if ORD[0] == 0 else pB
                    pHB = pB if ORD[0] == 0 else pA
                    ob = [ORD[0], ORD[1]]
                    tbk = [ORD[2], ORD[3]]

                    def stH1(t, hw=hw, g=g, ORD=ORD,
                             w_exB2=prev["exB2"], w_ev1=prev["ev1"],
                             w_first=(wave_toks if (hw == 0 and g == 0)
                                      else None)):
                        if w_first:
                            for tk in _flat(w_first):
                                t.wait_ge(sems[tk[0]], tk[1])
                        r = None
                        for c in range(6):
                            if c == 2 and w_exB2:
                                t.wait_ge(sems[w_exB2[0]], w_exB2[1])
                            if c == 4 and w_ev1:
                                t.wait_ge(sems[w_ev1[0]], w_ev1[1])
                            r = t.matmul(
                                bankap(ORD[c % 8], (c // 8) * 256, 256),
                                b16(O_KW + hw * 2048 + c * 128, 128),
                                b16(O_QW + hw * 1024 + g * 256, 256),
                                start=True, stop=True)
                        return r
                    add("tensor", stH1, waits=[prev["exB1"]])

                    if pend is not None:
                        tok_trp_p = add("tensor", pend["trp"],
                                        waits=[pend["ev1"]])
                        tok_ev2_p = add("vector", pend["ev2"],
                                        waits=[tok_trp_p])
                        prev["ev2"] = tok_ev2_p
                        tok_attn_done = tok_ev2_p
                        pend = None

                    SEQ2 = [8, 9, 10, 11, 12, 13, 6, 7, 14, 15]
                    sn = SEM_OF["tensor"]
                    cnt[sn] += 1
                    tok_sA1 = (sn, cnt[sn])
                    cnt[sn] += 1
                    tok_sA2 = (sn, cnt[sn])
                    cnt[sn] += 1
                    tok_sB1 = (sn, cnt[sn])

                    def stH2(t, hw=hw, g=g, ORD=ORD, SEQ2=SEQ2,
                             w_ev2=prev["ev2"]):
                        r = None
                        for i_, c in enumerate(SEQ2):
                            if c == 6 and w_ev2:
                                t.wait_ge(sems[w_ev2[0]], w_ev2[1])
                            r = t.matmul(
                                bankap(ORD[c % 8], (c // 8) * 256, 256),
                                b16(O_KW + hw * 2048 + c * 128, 128),
                                b16(O_QW + hw * 1024 + g * 256, 256),
                                start=True, stop=True)
                            if i_ in (1, 3, 5):
                                r.then_inc(sems["t"], 1)
                        return r
                    tok_sB2 = add("tensor", stH2)

                    def mkexp(src_ap, dst_off, w_tok, w_pv2=None):
                        def ex(sc, src_ap=src_ap, dst_off=dst_off,
                               w_pv2=w_pv2):
                            if w_pv2:
                                sc.wait_ge(sems[w_pv2[0]], w_pv2[1])
                            dst = bass.AP(ar, dst_off,
                                          [[AC, 128], [256, 2], [2048, 2],
                                           [1, 256]])
                            return sc.activation(dst, src_ap, AF.Exp,
                                                 bias=C_NSH, scale=C_SCL)
                        return add("scalar", ex, waits=[w_tok])
                    tok_exA1 = mkexp(pHA[:, 0:1024], pt_off, tok_sA1,
                                     prev["pv"][u % 2])
                    tok_exA2 = mkexp(pHA[:, 1024:2048], pt_off + 512, tok_sA2)
                    tok_exB1 = mkexp(pHB[:, 0:1024], pt_off + 1024, tok_sB1)
                    tok_exB2 = mkexp(pHB[:, 1024:2048], pt_off + 1536, tok_sB2)

                    GROUPS = [([0, 1, 8, 9], None), ([2, 3, 10, 11], tok_exA2),
                              ([4, 5, 12, 13], tok_exB1),
                              ([6, 7, 14, 15], tok_exB2)]

                    def pv(t, hw=hw, pt_off=pt_off, ob=ob, GROUPS=GROUPS):
                        r = None
                        for gi, (chs, wt) in enumerate(GROUPS):
                            if wt is not None:
                                t.wait_ge(sems[wt[0]], wt[1])
                            for qh in range(2):
                                for ci, c in enumerate(chs):
                                    r = t.matmul(
                                        bankap(ob[qh], 0, 129),
                                        b16(pt_off + c * 256 + qh * 128, 128),
                                        b16(O_VW + (c * 8 + hw) * 129, 129),
                                        start=(gi == 0 and ci == 0),
                                        stop=(gi == 3 and ci == 3))
                        return r
                    tok_pv = add("tensor", pv, waits=[tok_exA1])

                    def ev1a(v, ob=ob):
                        v.reciprocal(zin[:, 0:1], bankap(ob[0], 128, 1))
                        return v.reciprocal(zin[:, 1:2], bankap(ob[1], 128, 1))
                    add("vector", ev1a, waits=[tok_pv])

                    def ev1b(v, ob=ob):
                        v.tensor_scalar(b16(O_O2SB, 128), bankap(ob[0], 0, 128),
                                        zin[:, 0:1], None, OP.mult)
                        return v.tensor_scalar(b16(O_O2SB + 128, 128),
                                               bankap(ob[1], 0, 128),
                                               zin[:, 1:2], None, OP.mult)
                    tok_ev1 = add("vector", ev1b)

                    def trp(t, tbk=tbk):
                        r = None
                        for qh in range(2):
                            r = t.matmul(bankap(tbk[qh], 0, 128),
                                         b16(O_O2SB + qh * 128, 128),
                                         ident16[:], start=True, stop=True)
                        return r

                    def ev2(v, hw=hw, g=g, w=w, tbk=tbk):
                        r = None
                        for qh in range(2):
                            dst = b16(O_ATTN + (8 * w + hw) * 1024 + g * 256 +
                                      qh * 128, 128)
                            r = v.tensor_copy(dst, bankap(tbk[qh], 0, 128))
                        return r
                    pend = {"trp": trp, "ev2": ev2, "ev1": tok_ev1}

                    prev["exB1"] = tok_exB1
                    prev["exB2"] = tok_exB2
                    prev["ev1"] = tok_ev1
                    prev["pv"][u % 2] = tok_pv
                    u_glob += 1

            if pend is not None:
                tok_trp_p = add("tensor", pend["trp"], waits=[pend["ev1"]])
                tok_ev2_p = add("vector", pend["ev2"], waits=[tok_trp_p])
                prev["ev2"] = tok_ev2_p
                tok_attn_done = tok_ev2_p
                pend = None

        fence()

        if debug and debug >= 3:
            def d3(sy, D):
                sy.dma_start(dbg["d_kw"], b16(O_KW, 16384)).then_inc(D(), 16)
                sy.dma_start(dbg["d_vw"], b16(O_VW, 16512)).then_inc(D(), 16)
                sy.dma_start(dbg["d_qw"], b16(O_QW, 8192)).then_inc(D(), 16)
                sy.dma_start(dbg["d_attn"], b16(O_ATTN, 16384)
                             ).then_inc(D(), 16)
            add("sync", d3, n_dma=4)

        if PH < 5 and not PH_OFF[0]:
            def ydummy5(sy, D):
                for r_ in range(8):
                    sy.dma_start(y[r_ * 128:(r_ + 1) * 128, :], xf[0][:]
                                 ).then_inc(D(), 16)
            add("sync", ydummy5, n_dma=8)
            ops["sync"].append(("w", None, [(f"dq{li}", 16 * ((dq_next[0] - 1 - li) // 58 + 1)) for li in range(min(58, dq_next[0]))], None))
            PH_OFF[0] = True
        def ldwo_g(g_, D):
            g_.dma_start(b16(O_WOA, 16384), w_in["woa"]).then_inc(D(), 16)
        tok_ldwo_g = add("gpsimd", ldwo_g, waits=[tok_attn_done], n_dma=1)

        def ldwo_c(g_, D):
            g_.dma_start(b16(O_WOB, 16384), w_in["wob"]).then_inc(D(), 16)
        tok_ldwo_c = add("gpsimd", ldwo_c, n_dma=1)

        bank_tok = [None] * 8
        tok_xo = [None] * 8
        for tt in range(8):
            def xo(sy, D, tt=tt):
                sy.dma_start(xf[tt % 2][:], xc[tt * 128:(tt + 1) * 128, :]
                             ).then_inc(D(), 16)
            tok_xo[tt] = add("sync", xo, n_dma=1)

        wo_ev = {}
        for tt in range(8):
            for og in range(4):
                b_ = (tt % 2) * 4 + og
                woff = O_WOA if og < 2 else O_WOB

                def mm(t, tt=tt, og=og, b_=b_, woff=woff):
                    for fc in range(16):
                        t.matmul(bankap(b_),
                                 b16(O_ATTN + fc * 1024 + tt * 128, 128),
                                 b16(woff + fc * 1024 + (og % 2) * 512, 512),
                                 start=(fc == 0), stop=False)
                    return t.matmul(bankap(b_), ones16[:],
                                    bro16[0:1, og * 512:og * 512 + 512],
                                    start=False, stop=True)
                tok_mm = add("tensor", mm,
                             waits=[tok_ldwo_g if og < 2 else tok_ldwo_c,
                                    bank_tok[b_]])
                last_wo_mm = tok_mm

                def ev(v, tt=tt, og=og, b_=b_):
                    return v.tensor_tensor(
                        f32v(O_X2, 16384)[:, tt * 2048 + og * 512:
                                          tt * 2048 + og * 512 + 512],
                        bankap(b_), xf[tt % 2][:, og * 512:og * 512 + 512],
                        OP.add)
                tok_e = add("vector", ev, waits=[tok_mm, tok_xo[tt]])
                bank_tok[b_] = tok_e
                wo_ev[(tt, og)] = tok_e
        if not PH_OFF[0]:
            ph5_sync = [o for o in ops["sync"] if o[0] == "dma"][-8:]
            for tt in range(2, 8):
                ph5_sync[tt][2].append(wo_ev[(tt - 2, 3)])

        fence()
        if debug and debug >= 4:
            def d4(sy, D):
                sy.dma_start(dbg["d_x2"], b16(O_X2, 32768).bitcast(F32) if False
                             else f32v(O_X2, 16384)).then_inc(D(), 16)
            add("sync", d4, n_dma=1)

        if PH < 6 and not PH_OFF[0]:
            def ydummy6(sy, D):
                for r_ in range(8):
                    sy.dma_start(y[r_ * 128:(r_ + 1) * 128, :], xf[0][:]
                                 ).then_inc(D(), 16)
            add("sync", ydummy6, n_dma=8)
            ops["sync"].append(("w", None, [(f"dq{li}", 16 * ((dq_next[0] - 1 - li) // 58 + 1)) for li in range(min(58, dq_next[0]))], None))
            PH_OFF[0] = True
        NT2 = 8
        ev2_toks, _ = emit_ln(
            lambda i: f32v(O_X2, 16384)[:, i * 2048:(i + 1) * 2048],
            NT2, None, O_X2LN, 1024, O_TOKB, 1)

        fence()
        if debug and debug >= 5:
            def d5(sy, D):
                sy.dma_start(dbg["d_x2ln"], b16(O_X2LN, 16384)
                             ).then_inc(D(), 16)
            add("sync", d5, n_dma=1)

        if PH < 7 and not PH_OFF[0]:
            def ydummy7(sy, D):
                for r_ in range(8):
                    sy.dma_start(y[r_ * 128:(r_ + 1) * 128, :], xf[0][:]
                                 ).then_inc(D(), 16)
            add("sync", ydummy7, n_dma=8)
            ops["sync"].append(("w", None, [(f"dq{li}", 16 * ((dq_next[0] - 1 - li) // 58 + 1)) for li in range(min(58, dq_next[0]))], None))
            PH_OFF[0] = True
        def ldffn1(g_, D):
            g_.dma_start(b16(O_WA, 4096), w_in["wa"]).then_inc(D(), 16)
            g_.dma_start(b16(O_WB, 4096), w_in["wb"]).then_inc(D(), 16)
            g_.dma_start(b16(O_WR1, 1024), w_in["wr1"]).then_inc(D(), 16)
        tok_ldffn1 = add("gpsimd", ldffn1, n_dma=3)

        def ldwf_g(g_, D):
            g_.dma_start(b16(O_WFFN, 16384), w_in["wfa"]).then_inc(D(), 16)
        tok_wfa_g = add("gpsimd", ldwf_g, waits=[last_wo_mm], n_dma=1)
        tok_wfa_s = None

        bank_tok = [None] * 8
        tok_hev = []
        for g in range(2):
            def mm(t, g=g):
                for fc in range(16):
                    r = t.matmul(bankap(g)[0:64, :], b16(O_WR1 + fc * 64, 64),
                                 b16(O_X2LN + fc * 1024 + g * 512, 512),
                                 start=(fc == 0), stop=(fc == 15))
                return r
            tok_mm = add("tensor", mm, waits=[tok_ldffn1])

            def ev(sc, g=g):
                return sc.activation(
                    ar[0:64, O_HSB + g * 512:O_HSB + g * 512 + 512],
                    bankap(g)[0:64, :], AF.Silu,
                    bias=bias_pp[0:64, BC_R1:BC_R1 + 1])
            tok_hev.append(add("scalar", ev, waits=[tok_mm]))

        tok_lg = []
        for g in range(2):
            def mm(t, g=g):
                return t.matmul(bankap(2 + g)[0:32, :], wr2_sb[:],
                                ar[0:64, O_HSB + g * 512:O_HSB + g * 512 + 512],
                                start=True, stop=True)
            tok_mm = add("tensor", mm, waits=[tok_hev[g]])

            def ev(v, g=g):
                return v.tensor_scalar(
                    f32v(O_LGT, 1024)[0:32, g * 512:g * 512 + 512],
                    bankap(2 + g)[0:32, :],
                    bias_pp[0:32, 67:68], None, OP.add)
            tok_lg.append(add("vector", ev, waits=[tok_mm, t_cb]))

        tok_lgt = []
        for tt in range(8):
            def mm(t, tt=tt):
                return t.matmul(bankap(4 + tt % 4, 0, 16)[0:128, :] if False
                                else bankap(4 + tt % 4, (tt // 4) * 16, 16),
                                f32v(O_LGT, 1024)[0:16, tt * 128:(tt + 1) * 128],
                                identf[0:16, 0:16], start=True, stop=True)
            tok_mm = add("tensor", mm, waits=[tok_lg[0], tok_lg[1],
                                              bank_tok[4 + tt % 4]])

            def ev(v, tt=tt):
                return v.tensor_copy(lgt_tok[:, tt * 16:(tt + 1) * 16],
                                     bankap(4 + tt % 4, (tt // 4) * 16, 16))
            tok_e = add("vector", ev, waits=[tok_mm])
            bank_tok[4 + tt % 4] = tok_e
            tok_lgt.append(tok_e)

        tok_coft = []
        for tt in range(8):
            lg_sl = lgt_tok[:, tt * 16:(tt + 1) * 16]

            def v1a(v, lg_sl=lg_sl):
                return v.max(out=m8[:], in_=lg_sl)
            t_v1a = add("vector", v1a, waits=[tok_lgt[tt]])

            def v1(v):
                return v.tensor_tensor(tks[:, 0:1], m8[:, 1:2], m8[:, 0:1],
                                       OP.subtract)
            t_v1 = add("vector", v1, waits=[t_v1a])

            def s1(sc):
                return sc.activation(tks[:, 1:2], tks[:, 0:1], AF.Sigmoid)
            t_s1 = add("scalar", s1, waits=[t_v1])

            def v2a(v, lg_sl=lg_sl):
                v.tensor_scalar(tks[:, 2:3], tks[:, 1:2], C_NEG2, C_ONE,
                                OP.mult, OP.add)
                v.tensor_scalar(msk[:, 0:16], lg_sl, m8[:, 1:2], None, OP.is_gt)
                return v.tensor_scalar(msk[:, 16:32], lg_sl, m8[:, 2:3],
                                       None, OP.is_gt)
            add("vector", v2a, waits=[t_s1])

            def v2b(v):
                v.tensor_scalar(msk[:, 32:48], msk[:, 0:16], tks[:, 2:3],
                                None, OP.mult)
                return v.tensor_scalar(msk[:, 48:64], msk[:, 16:32], tks[:, 1:2],
                                       None, OP.mult)
            add("vector", v2b)

            def v2c(v, tt=tt):
                return v.tensor_tensor(coef_tok[:, tt * 16:(tt + 1) * 16],
                                       msk[:, 32:48], msk[:, 48:64], OP.add)
            tok_coft.append(add("vector", v2c))

        tok_cfm = []
        for tt in range(8):
            def mm(t, tt=tt):
                return t.matmul(bankap(tt % 4, (tt // 4) * 128, 128)[0:16, :],
                                coef_tok[:, tt * 16:(tt + 1) * 16],
                                identf[:], start=True, stop=True)
            tok_mm = add("tensor", mm, waits=[tok_coft[tt], bank_tok[tt % 4]])

            def ev(v, tt=tt):
                return v.tensor_copy(
                    f32v(O_CFM, 1024)[0:16, tt * 128:(tt + 1) * 128],
                    bankap(tt % 4, (tt // 4) * 128, 128)[0:16, :])
            tok_e = add("vector", ev, waits=[tok_mm])
            bank_tok[tt % 4] = tok_e
            tok_cfm.append(tok_e)

        def cfcast(v):
            return v.tensor_copy(ar[0:16, O_CFMB:O_CFMB + 1024],
                                 f32v(O_CFM, 1024)[0:16, :])
        tok_cfb = add("vector", cfcast, waits=tok_cfm + tok_hev)

        tok_cexp = []
        for erc in range(2):
            for g in range(2):
                b_ = 4 + (erc * 2 + g) % 4

                def mm(t, erc=erc, g=g, b_=b_):
                    return t.matmul(bankap(b_),
                                    emat16[:, erc * 128:(erc + 1) * 128],
                                    ar[0:16, O_CFMB + g * 512:O_CFMB + g * 512 + 512],
                                    start=True, stop=True)
                tok_mm = add("tensor", mm, waits=[tok_cfb, bank_tok[b_]])

                def ev(v, erc=erc, g=g, b_=b_):
                    return v.tensor_copy(
                        b16(O_CEXP + erc * 1024 + g * 512, 512), bankap(b_))
                tok_e = add("vector", ev, waits=[tok_mm])
                bank_tok[b_] = tok_e
                tok_cexp.append(tok_e)

        tok_wxa = []
        for erc in range(2):
            for g in range(2):
                b_ = (erc * 2 + g) % 4

                def mm(t, erc=erc, g=g, b_=b_):
                    for fc in range(16):
                        r = t.matmul(bankap(b_),
                                     b16(O_WA + fc * 256 + erc * 128, 128),
                                     b16(O_X2LN + fc * 1024 + g * 512, 512),
                                     start=(fc == 0), stop=(fc == 15))
                    return r
                tok_mm = add("tensor", mm, waits=[tok_ldffn1, bank_tok[b_]])

                def ev(v, erc=erc, g=g, b_=b_):
                    return v.tensor_copy(b16(O_XA + erc * 1024 + g * 512, 512),
                                         bankap(b_))
                tok_e = add("vector", ev, waits=[tok_mm])
                bank_tok[b_] = tok_e

                def wx(v, erc=erc, g=g):
                    off = erc * 1024 + g * 512
                    return v.tensor_tensor(b16(O_WXA + off, 512),
                                           b16(O_XA + off, 512),
                                           b16(O_CEXP + off, 512), OP.mult)
                tok_wxa.append(add("vector", wx,
                                   waits=[tok_cexp[erc * 2 + g]]))

        fence()
        if debug and debug >= 6:
            def d6(sy, D):
                sy.dma_start(dbg["d_lgt"], lgt_tok[:]).then_inc(D(), 16)
                sy.dma_start(dbg["d_coef"], coef_tok[:]).then_inc(D(), 16)
                sy.dma_start(dbg["d_xa"], b16(O_XA, 2048)).then_inc(D(), 16)
                sy.dma_start(dbg["d_wxa"], b16(O_WXA, 2048)).then_inc(D(), 16)
                sy.dma_start(dbg["d_h"], b16(O_HSB, 1024)).then_inc(D(), 16)
            add("sync", d6, n_dma=5)

        if PH < 8 and not PH_OFF[0]:
            def ydummy8(sy, D):
                for r_ in range(8):
                    sy.dma_start(y[r_ * 128:(r_ + 1) * 128, :], xf[0][:]
                                 ).then_inc(D(), 16)
            add("sync", ydummy8, n_dma=8)
            ops["sync"].append(("w", None, [(f"dq{li}", 16 * ((dq_next[0] - 1 - li) // 58 + 1)) for li in range(min(58, dq_next[0]))], None))
            PH_OFF[0] = True
        bank_tok = [None] * 8
        out_tok = [None] * 8
        tok_wfb = None
        last_oga_mm = None
        n_ydma = 0
        for og in range(4):
            if og == 2:
                def ldwfb_g(g_, D):
                    g_.dma_start(b16(O_WFFN, 16384), w_in["wfb"]
                                 ).then_inc(D(), 16)
                tok_wfb_g = add("gpsimd", ldwfb_g, waits=[last_oga_mm],
                                n_dma=1)
                tok_wfb_s = None
            for tt in range(8):
                b_ = (og * 8 + tt) % 8
                ob_ = (og * 8 + tt) % 8

                def mm(t, og=og, tt=tt, b_=b_):
                    for fc in range(16):
                        t.matmul(bankap(b_),
                                 b16(O_X2LN + fc * 1024 + tt * 128, 128),
                                 b16(O_WFFN + fc * 1024 + (og % 2) * 512, 512),
                                 start=(fc == 0), stop=False)
                    for erc in range(2):
                        t.matmul(bankap(b_),
                                 b16(O_WXA + erc * 1024 + tt * 128, 128),
                                 b16(O_WB + erc * 2048 + og * 512, 512),
                                 start=False, stop=False)
                    return t.matmul(bankap(b_), ones16[:],
                                    brf16[0:1, og * 512:og * 512 + 512],
                                    start=False, stop=True)
                tok_mm = add("tensor", mm,
                             waits=([tok_wfa_g, tok_wfa_s] if og < 2 else
                                    [tok_wfb_g, tok_wfb_s]) +
                             [bank_tok[b_]] +
                             (tok_wxa if (og == 0 and tt == 0) else []))
                if og == 1 and tt == 7:
                    last_oga_mm = tok_mm

                def ev(v, og=og, tt=tt, b_=b_, ob_=ob_):
                    return v.tensor_tensor(
                        xf[ob_ // 4][:, (ob_ % 4) * 512:(ob_ % 4) * 512 + 512],
                        bankap(b_),
                        f32v(O_X2, 16384)[:, tt * 2048 + og * 512:
                                          tt * 2048 + og * 512 + 512],
                        OP.add)
                tok_e = add("vector", ev, waits=[tok_mm, out_tok[ob_]])
                bank_tok[b_] = tok_e

                def dy(sy, D, og=og, tt=tt, ob_=ob_):
                    sy.dma_start(y[tt * 128:(tt + 1) * 128,
                                   og * 512:og * 512 + 512],
                                 xf[ob_ // 4][:, (ob_ % 4) * 512:
                                              (ob_ % 4) * 512 + 512]
                                 ).then_inc(D(), 16)
                tok_d = add("sync", dy, waits=[tok_e], n_dma=1)
                out_tok[ob_] = tok_d
                n_ydma += 1

        ops["sync"].append(("w", None,
                            [(f"dq{li}", 16 * ((dq_next[0] - 1 - li) // 58 + 1))
                             for li in range(min(58, dq_next[0]))] +
                            [(f"dq{58 + li}", 16) for li in range(dq_next[1])],
                            None))

        for eng in ENGS:
            deco = getattr(blk, eng)
            myops = ops[eng]

            def runner(hh, myops=myops, eng=eng):
                sn = SEM_OF.get(eng)
                for opi, (kind, fn, waits, plan) in enumerate(myops):
                    if any(w is None or not isinstance(w, tuple) for w in waits):
                        raise RuntimeError(f"bad waits {eng} op{opi}: {waits}")
                    for wsn, val in waits:
                        hh.wait_ge(sems[wsn], val)
                    if kind == "cmp":
                        if plan is not None and plan > 0:
                            hh.wait_ge(sems[sn], plan)
                        instr = fn(hh)
                        instr.then_inc(sems[sn], 1)
                    elif kind == "dma":
                        it = iter(plan)

                        def D(hh=hh, it=it):
                            lane, frm = next(it)
                            if frm > 0:
                                hh.wait_ge(sems[lane], frm)
                            return sems[lane]
                        fn(hh, D)
            deco(runner)

    return nc, dbg


def _chunk128(a):
    r, c = a.shape
    assert r % 128 == 0
    return np.ascontiguousarray(
        a.reshape(r // 128, 128, c).transpose(1, 0, 2).reshape(128, -1))


def _prep_shared(i):
    f = np.float32
    out = {}
    out["identf"] = np.eye(128, dtype=f)
    bp = np.zeros((128, 68), f)
    bp[:, BC_KV:BC_KV + 4] = i["L0_attn_bkv_d"].reshape(4, 128).T
    bp[:, BC_QD:BC_QD + 4] = i["L0_attn_bq_d"].reshape(4, 128).T
    bp[0:64, BC_KU:BC_KU + 16] = i["L0_attn_bk_u"].reshape(16, 64).T
    bp[0:64, BC_QU:BC_QU + 16] = i["L0_attn_bq_u"].reshape(16, 64).T
    bp[:, BC_RK:BC_RK + 8] = i["L0_attn_brk"].reshape(8, 128).T
    bp[0:64, BC_RQ:BC_RQ + 16] = i["L0_attn_brq"].reshape(16, 64).T
    bp[0:64, BC_R1] = i["L0_ffn_rb1"]
    bp[0:32, BC_R2] = i["L0_ffn_rb2"]
    bp[0:16, BC_G] = i["L0_ffn_gates"]
    out["bias_pp"] = bp
    out["brv"] = i["L0_attn_bv_u"].reshape(1, E).astype(f)
    out["bro"] = i["L0_attn_bo"].reshape(1, E).astype(f)
    out["brf"] = i["L0_ffn_bias"].reshape(1, E).astype(f)
    em = np.zeros((16, 256), f)
    for e_ in range(16):
        em[e_, e_ * 16:(e_ + 1) * 16] = 1.0
    out["emat"] = em
    rotm = np.zeros((128, 128), f)
    for blk0 in (0, 64):
        for d in range(32):
            rotm[blk0 + d + 32, blk0 + d] = -1.0
            rotm[blk0 + d, blk0 + d + 32] = 1.0
    out["rotm"] = rotm
    out["wkv"] = _chunk128(i["L0_attn_Wkv_d"].T.astype(f))
    out["wqd"] = _chunk128(i["L0_attn_Wq_d"].T.astype(f))
    wrkT = _chunk128(i["L0_attn_Wrk"].T.astype(f))
    wrk = np.zeros((128, 16384), f)
    for oc in range(8):
        for fc in range(16):
            wrk[:, oc * 2048 + fc * 128:oc * 2048 + (fc + 1) * 128] = \
                wrkT[:, fc * 1024 + oc * 128:fc * 1024 + (oc + 1) * 128]
    out["wrk"] = np.ascontiguousarray(wrk)
    out["wku"] = _chunk128(i["L0_attn_Wk_u"].T.astype(f))
    out["wqu"] = _chunk128(i["L0_attn_Wq_u"].T.astype(f))
    out["wvu"] = _chunk128(i["L0_attn_Wv_u"].T.astype(f))
    out["wrq"] = _chunk128(i["L0_attn_Wrq"].T.astype(f))
    woT = _chunk128(i["L0_attn_Wo"].T.astype(f))
    out["woa"] = np.ascontiguousarray(
        woT.reshape(128, 16, 2048)[:, :, 0:1024].reshape(128, -1))
    out["wob"] = np.ascontiguousarray(
        woT.reshape(128, 16, 2048)[:, :, 1024:2048].reshape(128, -1))
    wfT = _chunk128(i["L0_ffn_W"].T.astype(f))
    out["wfa"] = np.ascontiguousarray(
        wfT.reshape(128, 16, 2048)[:, :, 0:1024].reshape(128, -1))
    out["wfb"] = np.ascontiguousarray(
        wfT.reshape(128, 16, 2048)[:, :, 1024:2048].reshape(128, -1))
    out["wa"] = _chunk128(np.ascontiguousarray(
        i["L0_ffn_A"].reshape(256, E).T.astype(f)))
    out["wb"] = _chunk128(np.ascontiguousarray(
        i["L0_ffn_B"].transpose(0, 2, 1).reshape(256, E).astype(f)))
    out["wr1"] = _chunk128(i["L0_ffn_rw1"].T.astype(f))
    out["wr2"] = np.ascontiguousarray(i["L0_ffn_rw2"].T.astype(f))
    return out


def _rope_tables(pos):
    inv = (1.0 / 10000.0 ** (np.arange(0, R, 2, dtype=np.float64) / R)
           ).astype(np.float32)
    t = pos.astype(np.float32)
    fr = np.outer(t, inv)
    emb = np.concatenate([fr, fr], axis=-1)
    cos = np.cos(emb).astype(np.float32)
    sin = np.sin(emb).astype(np.float32)
    cc = np.zeros((128, len(pos)), np.float32)
    sg = np.zeros((128, len(pos)), np.float32)
    for p in range(128):
        d = p % 64
        cc[p] = cos[:, d]
        sg[p] = sin[:, d]
    return cc, sg


def kernel(**inputs):
    shared = _prep_shared(inputs)
    x = np.asarray(inputs["L0_attn_x"], dtype=np.float32)
    nc, _ = build(debug=False)
    in_maps = []
    for c in range(8):
        b_, th = c // 2, c % 2
        if th == 0:
            xcore = np.ascontiguousarray(x[b_])
            pos = np.arange(S)
        else:
            xcore = np.ascontiguousarray(
                np.concatenate([x[b_, TOK:], x[b_, :TOK]], axis=0))
            pos = np.concatenate([np.arange(TOK, S), np.arange(TOK)])
        cc, sg = _rope_tables(pos)
        m = dict(shared)
        m["xc"] = xcore
        m["cc"] = cc
        m["sgn"] = sg
        in_maps.append(m)
    res = run_bass_kernel_spmd(nc, in_maps, core_ids=list(range(8)))
    out = np.empty((B, S, E), dtype=np.float32)
    for c in range(8):
        b_, th = c // 2, c % 2
        out[b_, th * TOK:(th + 1) * TOK, :] = res.results[c]["y"]
    return out
